# version 22
# baseline (speedup 1.0000x reference)
import math
import numpy as np
import ml_dtypes
import concourse.bass as bass
import concourse.mybir as mybir
from concourse.bass_utils import run_bass_kernel_spmd
from contextlib import ExitStack

F32 = mybir.dt.float32
BF16 = mybir.dt.bfloat16
AF = mybir.ActivationFunctionType
ALU = mybir.AluOpType
AX = mybir.AxisListType

NTOK = 16384
E = 1024
EC = E // 128
D = 64
H = 4096
HC = H // 128
OWN = 2048
NT = NTOK // 128
OT = OWN // 128
CTX_SCALE = math.sqrt(3.0 / D)

_CACHE = {}


def build():
    nc = bass.Bass("TRN2", target_bir_lowering=False)
    x = nc.dram_tensor("x", [NTOK, E], F32, kind="ExternalInput").ap()
    wq = nc.dram_tensor("wq", [D, E], F32, kind="ExternalInput").ap()
    wk = nc.dram_tensor("wk", [D, E], F32, kind="ExternalInput").ap()
    wv = nc.dram_tensor("wv", [D, E], F32, kind="ExternalInput").ap()
    wo = nc.dram_tensor("wo", [E, D], F32, kind="ExternalInput").ap()
    w1 = nc.dram_tensor("w1", [H, E], F32, kind="ExternalInput").ap()
    w2 = nc.dram_tensor("w2", [H, E], F32, kind="ExternalInput").ap()
    h1 = nc.dram_tensor("h1", [H, H], F32, kind="ExternalInput").ap()
    h2 = nc.dram_tensor("h2", [H, H], F32, kind="ExternalInput").ap()
    w3 = nc.dram_tensor("w3", [E, H], F32, kind="ExternalInput").ap()
    s1v = nc.dram_tensor("s1v", [32, 128], F32, kind="ExternalInput").ap()
    s2v = nc.dram_tensor("s2v", [32, 128], F32, kind="ExternalInput").ap()
    sh1v = nc.dram_tensor("sh1v", [32, 128], F32, kind="ExternalInput").ap()
    sh2v = nc.dram_tensor("sh2v", [32, 128], F32, kind="ExternalInput").ap()
    s3v = nc.dram_tensor("s3v", [8, 128], F32, kind="ExternalInput").ap()
    y = nc.dram_tensor("y", [OWN, E], F32, kind="ExternalOutput").ap()
    x1s = nc.dram_tensor("x1s", [OWN, E], F32).ap()
    wt12 = nc.dram_tensor("wt12", [HC, 128, 2048], BF16).ap()
    h1t = nc.dram_tensor("h1t", [HC, 128, 4096], BF16).ap()
    h2t = nc.dram_tensor("h2t", [HC, 128, 4096], BF16).ap()
    w3t = nc.dram_tensor("w3t", [EC, 128, 4096], BF16).ap()

    ctx = ExitStack()
    EE = ctx.enter_context
    IDB = EE(nc.sbuf_tensor("IDB", [128, 128], BF16))
    IDF = EE(nc.sbuf_tensor("IDF", [128, 128], F32))
    KVQT = EE(nc.sbuf_tensor("KVQT", [128, EC * 192], BF16))
    WOT = EE(nc.sbuf_tensor("WOT", [D, E], BF16))
    SC1 = EE(nc.sbuf_tensor("SC1", [128, 32], F32))
    SC2 = EE(nc.sbuf_tensor("SC2", [128, 32], F32))
    SCH1 = EE(nc.sbuf_tensor("SCH1", [128, 32], F32))
    SCH2 = EE(nc.sbuf_tensor("SCH2", [128, 32], F32))
    SC3 = EE(nc.sbuf_tensor("SC3", [128, 8], F32))
    CT = EE(nc.sbuf_tensor("CT", [D, OWN], BF16))
    CST = EE(nc.sbuf_tensor("CST", [128, 4], F32))
    ctxA = ExitStack()
    EA = ctxA.enter_context
    WKn = EA(nc.sbuf_tensor("WKn", [D, E], BF16))
    WVn = EA(nc.sbuf_tensor("WVn", [D, E], BF16))
    WQn = EA(nc.sbuf_tensor("WQn", [D, E], BF16))
    WOn = EA(nc.sbuf_tensor("WOn", [128, 512], BF16))
    SCL = [EA(nc.sbuf_tensor(f"SCL{i}", [32, 128], F32)) for i in range(4)]
    SCL3 = EA(nc.sbuf_tensor("SCLW3", [8, 128], F32))
    XB = EA(nc.sbuf_tensor("XB", [128, 4 * E], BF16))
    XT = EA(nc.sbuf_tensor("XT", [128, 4 * E], BF16))
    KVB = EA(nc.sbuf_tensor("KVB", [128, 4 * 192], BF16))
    QN = EA(nc.sbuf_tensor("QN", [128, OT * D], BF16))
    ENF = EA(nc.sbuf_tensor("ENF", [D, D], F32))
    ENB = EA(nc.sbuf_tensor("ENB", [D, D], BF16))
    QT = EA(nc.sbuf_tensor("QT", [D, OWN], BF16))
    PT = [EE(nc.psum_tensor("PT1", [128, 1024], BF16)),
          EE(nc.psum_tensor("PT2", [128, 1024], BF16))]
    PP = [EE(nc.psum_tensor("PA", [128, 1024], F32)),
          EE(nc.psum_tensor("PB", [128, 1024], F32))]
    PEn = EE(nc.psum_tensor("PEn", [D, D], F32))
    PM = EE(nc.psum_tensor("PM", [128, 512], F32))
    SL = EE(nc.semaphore("SL"))
    SGP = [EE(nc.semaphore("SGP0")), EE(nc.semaphore("SGP1")),
           EE(nc.semaphore("SGP2")), EE(nc.semaphore("SGP3"))]
    SYL = [EE(nc.semaphore("SYL0")), EE(nc.semaphore("SYL1"))]
    SR = [EE(nc.semaphore("SR0")), EE(nc.semaphore("SR1"))]
    SYS = [EE(nc.semaphore("SYS0")), EE(nc.semaphore("SYS1"))]
    SWS = [EE(nc.semaphore("SWS0")), EE(nc.semaphore("SWS1"))]
    SUS = [EE(nc.semaphore("SUS0")), EE(nc.semaphore("SUS1"))]
    ST = EE(nc.semaphore("ST"))
    SS = EE(nc.semaphore("SS"))
    SV = EE(nc.semaphore("SV"))
    SP = EE(nc.semaphore("SP"))

    Q = {k: [] for k in "gystv"}
    C = {"l": 0, "g0": 0, "g1": 0, "g2": 0, "g3": 0,
         "r0": 0, "r1": 0,
         "y0": 0, "y1": 0, "z0": 0, "z1": 0,
         "w0": 0, "w1": 0, "u0": 0, "u1": 0,
         "s": 0, "t": 0, "v": 0, "p": 0}
    SEM = {"l": SL, "g0": SGP[0], "g1": SGP[1], "g2": SGP[2], "g3": SGP[3],
           "r0": SR[0], "r1": SR[1],
           "y0": SYL[0], "y1": SYL[1],
           "z0": SYS[0], "z1": SYS[1],
           "w0": SWS[0], "w1": SWS[1], "u0": SUS[0], "u1": SUS[1],
           "s": SS, "t": ST, "v": SV, "p": SP}

    def em(q, f):
        Q[q].append(f)

    def replay():
        with nc.Block() as block:
            @block.gpsimd
            def _(g):
                for f in Q['g']:
                    f(g)

            @block.sync
            def _(s_):
                for f in Q['y']:
                    f(s_)

            @block.tensor
            def _(t):
                for f in Q['t']:
                    f(t)

            @block.scalar
            def _(s):
                for f in Q['s']:
                    f(s)

            @block.vector
            def _(v):
                for f in Q['v']:
                    f(v)
        for k in Q:
            Q[k] = []

    def w(q, src, val):
        if val > 0:
            em(q, lambda e, s=SEM[src], v=val: e.wait_ge(s, v))

    def inc(q, n=1):
        C[q] += n
        return C[q]

    def fi(ins, fin, sem=None):
        if fin:
            ins.then_inc(sem if sem is not None else ST, 1)
        return ins

    QENG = {"l": "g", "g0": "g", "g1": "g", "g2": "g", "g3": "g",
            "r0": "y", "r1": "y",
            "y0": "y", "y1": "y",
            "z0": "y", "z1": "y",
            "w0": "s", "w1": "s", "u0": "s", "u1": "s",
            "s": "s", "t": "t", "v": "v", "p": "g"}

    def barrier():
        snap = dict(C)
        for q in "gystv":
            for o in C:
                if QENG[o] != q:
                    w(q, o, snap[o])

    def light_handoff(pe_gate):
        w('t', 's', C['s'])
        w('t', 'v', C['v'])
        w('s', 'w0', C['w0'])
        w('s', 'w1', C['w1'])
        w('v', 'w0', C['w0'])
        w('v', 'w1', C['w1'])
        w('g', 't', pe_gate)
        w('g', 'w0', C['w0'])
        w('g', 'w1', C['w1'])

    def aff_ident(t_):
        em('g', lambda g, t_=t_: g.memset(t_[:], 0.0).then_inc(SP, 1))
        v_ = inc('p', 1)
        w('g', 'p', v_)
        em('g', lambda g, t_=t_: g.affine_select(
            out=t_[:], in_=t_[:], compare_op=ALU.not_equal, fill=1.0, base=0,
            pattern=[[-1, 128]], channel_multiplier=1).then_inc(SP, 1))
        inc('p', 1)

    em('g', lambda g: g.memset(CST[:, 0:1], 1e-5))
    em('g', lambda g: g.memset(CST[:, 1:2], 1.0 / E))
    em('g', lambda g: g.memset(CST[:, 2:3], CTX_SCALE).then_inc(SP, 1))
    inc('p', 1)
    aff_ident(IDB)
    aff_ident(IDF)
    em('g', lambda g: g.dma_start(out=WKn[:], in_=wk).then_inc(SL, 16)); inc('l', 16)
    em('g', lambda g: g.dma_start(out=WVn[:], in_=wv).then_inc(SL, 16)); inc('l', 16)
    em('g', lambda g: g.dma_start(out=WQn[:], in_=wq).then_inc(SL, 16)); inc('l', 16)
    em('g', lambda g: g.dma_start(
        out=WOn[:].rearrange("p (a d) -> p a d", a=8),
        in_=wo.rearrange("(a p) d -> p a d", p=128)).then_inc(SL, 16)); inc('l', 16)
    for i, (dst, src) in enumerate([(SCL[0], s1v), (SCL[1], s2v), (SCL[2], sh1v),
                                    (SCL[3], sh2v), (SCL3, s3v)]):
        em('g', lambda g, d_=dst, s_=src: g.dma_start(out=d_[:], in_=s_).then_inc(SL, 16))
        inc('l', 16)
    g_loads = C['l']

    w('t', 'l', g_loads)
    w('t', 'p', C['p'])
    for wi, (wn, off) in enumerate([(WKn, 0), (WVn, 64), (WQn, 128)]):
        for c in range(EC):
            em('t', lambda t, wn=wn, c=c: fi(t.transpose(
                PT[0][0:128, c * 64:(c + 1) * 64], wn[0:D, c * 128:(c + 1) * 128],
                IDB[0:D, 0:D]), c == EC - 1))
        v_tp = inc('t')
        w('s', 't', v_tp)
        em('s', lambda s, off=off: s.copy(
            KVQT[:].rearrange("p (c t) -> p c t", c=EC)[:, :, off:off + 64],
            PT[0][:, 0:512].rearrange("p (c t) -> p c t", t=64)).then_inc(SS, 1))
        v_ev = inc('s')
        w('t', 's', v_ev)
    for c in range(EC):
        em('t', lambda t, c=c: fi(t.transpose(
            PT[1][0:D, c * 128:(c + 1) * 128], WOn[:, c * 64:(c + 1) * 64],
            IDB[:]), c == EC - 1))
    v_wot = inc('t')
    w('s', 't', v_wot)
    em('s', lambda s: s.copy(WOT[:], PT[1][0:D, 0:1024]).then_inc(SS, 1)); inc('s')
    for i in range(4):
        em('t', lambda t, i=i: t.transpose(
            PM[0:128, i * 32:(i + 1) * 32], SCL[i][0:32, :], IDF[0:32, 0:32]))
    em('t', lambda t: fi(t.transpose(PM[0:128, 4 * 32:4 * 32 + 8], SCL3[0:8, :],
                                     IDF[0:8, 0:8]), True))
    v_sct = inc('t')
    w('s', 't', v_sct)
    for i, dst in enumerate([SC1, SC2, SCH1, SCH2]):
        em('s', lambda s, d_=dst, i=i: s.copy(d_[:], PM[:, i * 32:(i + 1) * 32]))
    em('s', lambda s: s.copy(SC3[:], PM[:, 128:136]).then_inc(SS, 1)); inc('s')
    barrier()

    xtp = {}; xte = {}; kvq = {}; kve = {}; emm = {}
    xb_dma = {}
    KVQP = [PP[0][:, 0:192], PP[0][:, 512:704],
            PP[1][:, 0:192], PP[1][:, 512:704]]
    for t_i in range(NT + 4):
        if t_i < NT:
            r4 = t_i % 4
            r2 = t_i % 2
            if t_i >= 4:
                w('g', 't', xtp[t_i - 4])
            em('g', lambda g, t_i=t_i, r4=r4: g.dma_start(
                out=XB[:, r4 * E:(r4 + 1) * E],
                in_=x[t_i * 128:(t_i + 1) * 128, :]).then_inc(SGP[r4], 16))
            xb_dma[t_i] = inc(f'g{r4}', 16)
            w('t', f'g{r4}', xb_dma[t_i])
            if t_i >= 2:
                w('t', 's', xte[t_i - 2])
            for c in range(EC):
                em('t', lambda t, r2=r2, r4=r4, c=c: fi(t.transpose(
                    PT[r2][:, c * 128:(c + 1) * 128],
                    XB[:, r4 * E + c * 128: r4 * E + (c + 1) * 128], IDB[:]),
                    c == EC - 1))
            xtp[t_i] = inc('t')
            w('s', 't', xtp[t_i])
            if t_i >= 4:
                w('s', 't', kvq[t_i - 4])
            em('s', lambda s, r2=r2, r4=r4: s.copy(
                XT[:, r4 * E:(r4 + 1) * E], PT[r2][:, 0:1024]).then_inc(SS, 1))
            xte[t_i] = inc('s')
        tj = t_i - 2
        if 0 <= tj < NT:
            rj = tj % 4
            W_ = 192 if tj < OT else 128
            w('t', 's', xte[tj])
            if tj >= 4:
                w('t', 'v', kve[tj - 4])
            for c in range(EC):
                em('t', lambda t, rj=rj, c=c, W_=W_: fi(t.matmul(
                    KVQP[rj][:, 0:W_],
                    XT[:, rj * E + c * 128: rj * E + (c + 1) * 128],
                    KVQT[:, c * 192: c * 192 + W_],
                    start=(c == 0), stop=(c == EC - 1)), c == EC - 1))
            kvq[tj] = inc('t')
            w('v', 't', kvq[tj])
            if tj >= 4:
                w('v', 't', emm[tj - 4])
            own_ = tj < OT
            em('v', lambda v, rj=rj, W_=W_, own_=own_: fi(v.tensor_copy(
                KVB[:, rj * 192: rj * 192 + W_], KVQP[rj][:, 0:W_]),
                not own_, SV))
            if own_:
                em('v', lambda v, tj=tj, rj=rj: fi(v.tensor_copy(
                    QN[:, tj * D:(tj + 1) * D], KVQP[rj][:, 128:192]), True, SV))
            kve[tj] = inc('v')
        tk = t_i - 4
        if 0 <= tk < NT:
            rk = tk % 4
            w('t', 'v', kve[tk])
            em('t', lambda t, rk=rk, tk=tk: t.matmul(
                PEn[:], KVB[:, rk * 192: rk * 192 + 64],
                KVB[:, rk * 192 + 64: rk * 192 + 128],
                start=(tk == 0), stop=(tk == NT - 1)).then_inc(ST, 1))
            emm[tk] = inc('t')
    barrier()

    em('s', lambda s: s.activation(ENF[:], PEn[:], AF.Copy, bias=0.0,
                                   scale=CST[0:D, 2:3]).then_inc(SS, 1))
    v_enf = inc('s')
    w('v', 's', v_enf)
    em('v', lambda v: v.tensor_copy(ENB[:], ENF[:]).then_inc(SV, 1)); v_enb = inc('v')
    for b in range(2):
        for k in range(8):
            ki = b * 8 + k
            em('t', lambda t, b=b, k=k, ki=ki: fi(t.transpose(
                PT[b][0:D, k * 128:(k + 1) * 128], QN[:, ki * D:(ki + 1) * D],
                IDB[:]), k == 7))
        v_qtp = inc('t')
        w('s', 't', v_qtp)
        em('s', lambda s, b=b: s.copy(
            QT[:, b * 1024:(b + 1) * 1024], PT[b][0:D, 0:1024]).then_inc(SS, 1))
        inc('s')
    v_qte = C['s']
    w('t', 'v', v_enb)
    w('t', 's', v_qte)
    for g_i in range(4):
        em('t', lambda t, g_i=g_i: t.matmul(
            PM[0:D, 0:512], ENB[:], QT[:, g_i * 512:(g_i + 1) * 512],
            start=True, stop=True).then_inc(ST, 1))
        v_ctm = inc('t')
        w('s', 't', v_ctm)
        em('s', lambda s, g_i=g_i: s.copy(
            CT[:, g_i * 512:(g_i + 1) * 512], PM[0:D, 0:512]).then_inc(SS, 1))
        v_cte = inc('s')
        w('t', 's', v_cte)
    barrier()

    def emit_ln(r, slot, dump=None):
        dump = TS1 if dump is None else dump
        w('s', 'v', C['v'])
        em('s', lambda s, slot=slot: s.activation(
            NM2[:, slot:slot + 1], NM[:, slot:slot + 1], AF.Copy, bias=0.0,
            scale=CST[:, 1:2]).then_inc(SS, 1))
        inc('s')
        em('s', lambda s, r=r, slot=slot, dump=dump: s.activation(
            dump[:, 0:1024], X1F[:, r * E:(r + 1) * E], AF.Square,
            bias=0.0, scale=1.0, accum_out=SSQ[:, slot:slot + 1]).then_inc(SS, 1))
        sq_ = inc('s')
        w('v', 's', sq_)
        em('v', lambda v, slot=slot: v.tensor_mul(
            MM[:, slot:slot + 1], NM2[:, slot:slot + 1],
            NM2[:, slot:slot + 1]).then_inc(SV, 1))
        vmm = inc('v')
        w('v', 'v', vmm)
        em('v', lambda v, slot=slot: v.scalar_tensor_tensor(
            VAR[:, slot:slot + 1], SSQ[:, slot:slot + 1], CST[:, 1:2],
            MM[:, slot:slot + 1], ALU.mult, ALU.subtract).then_inc(SV, 1))
        vv = inc('v')
        w('s', 'v', vv)
        em('s', lambda s, slot=slot: s.activation(
            SD[:, slot:slot + 1], VAR[:, slot:slot + 1], AF.Sqrt,
            bias=CST[:, 0:1], scale=1.0).then_inc(SS, 1))
        s2_ = inc('s')
        w('v', 's', s2_)
        em('v', lambda v, slot=slot: v.reciprocal(
            RS[:, slot:slot + 1], SD[:, slot:slot + 1]).then_inc(SV, 1))
        vr_ = inc('v')
        w('v', 'v', vr_)
        em('v', lambda v, slot=slot: v.tensor_scalar(
            NMR[:, slot:slot + 1], RS[:, slot:slot + 1], NM2[:, slot:slot + 1],
            -1.0, ALU.mult, ALU.mult).then_inc(SV, 1))
        vn_ = inc('v')
        w('v', 'v', vn_)
        em('v', lambda v, r=r, slot=slot: v.tensor_scalar(
            X1F[:, r * E:(r + 1) * E], X1F[:, r * E:(r + 1) * E],
            RS[:, slot:slot + 1], NMR[:, slot:slot + 1],
            ALU.mult, ALU.add).then_inc(SV, 1))
        return inc('v')

    def emit_b3(h):
        attn = {}; addv = {}; x1tp = {}; x1te = {}; xdma = {}; x1st = {}
        ysnap = {0: C['y0'], 1: C['y1']}
        zsnap = {0: C['z0'], 1: C['z1']}
        vsnap = C['v']
        for ti in range(OT // 2 + 1):
            tau = ti
            th = 8 * h + tau
            r = tau % 2
            if tau < 8:
                if tau >= 2:
                    w('y', 'v', addv[tau - 2])
                else:
                    w('y', f'y{r}', ysnap[r])
                    w('y', 'v', vsnap)
                em('y', lambda s_, th=th, r=r: s_.dma_start(
                    out=X1R[:, r * E:(r + 1) * E],
                    in_=x[th * 128:(th + 1) * 128, :]).then_inc(SYL[r], 16))
                xdma[tau] = inc(f'y{r}', 16)
                if tau >= 2:
                    w('t', 'v', addv[tau - 2])
                for g_i in range(2):
                    em('t', lambda t, th=th, r=r, g_i=g_i: fi(t.matmul(
                        PP[r][:, g_i * 512:(g_i + 1) * 512],
                        CT[:, th * 128:(th + 1) * 128],
                        WOT[:, g_i * 512:(g_i + 1) * 512],
                        start=True, stop=True), g_i == 1))
                attn[tau] = inc('t')
                w('v', 't', attn[tau])
                w('v', f'y{r}', xdma[tau])
                if tau >= 2:
                    w('v', f'z{r}', x1st[tau - 2])
                else:
                    w('v', f'z{r}', zsnap[r])
                em('v', lambda v, r=r: v.scalar_tensor_tensor(
                    X1F[:, r * E:(r + 1) * E], PP[r][:, 0:1024], 1.0,
                    X1R[:, r * E:(r + 1) * E], ALU.mult, ALU.add,
                    accum_out=NM[:, r:r + 1]).then_inc(SV, 1))
                inc('v')
                lnv = emit_ln(r, r)
                w('y', 'v', lnv)
                if tau >= 2:
                    w('y', f'z{r}', x1st[tau - 2])
                else:
                    w('y', f'z{r}', zsnap[r])
                em('y', lambda s_, th=th, r=r: s_.dma_start(
                    out=x1s[th * 128:(th + 1) * 128, :],
                    in_=X1F[:, r * E:(r + 1) * E]).then_inc(SYS[r], 16))
                x1st[tau] = inc(f'z{r}', 16)
                if tau >= 2:
                    w('v', 't', x1tp[tau - 2])
                w('v', 'v', lnv)
                em('v', lambda v, r=r: v.tensor_copy(
                    X1B[:, r * E:(r + 1) * E],
                    X1F[:, r * E:(r + 1) * E]).then_inc(SV, 1))
                addv[tau] = inc('v')
            tp = ti - 1
            if 0 <= tp < 8:
                rp = tp % 2
                w('t', 'v', addv[tp])
                if tp >= 2:
                    w('t', 's', x1te[tp - 2])
                for c in range(EC):
                    em('t', lambda t, rp=rp, c=c: fi(t.transpose(
                        PT[rp][:, c * 128:(c + 1) * 128],
                        X1B[:, rp * E + c * 128: rp * E + (c + 1) * 128],
                        IDB[:]), c == EC - 1))
                x1tp[tp] = inc('t')
                w('s', 't', x1tp[tp])
                em('s', lambda s, rp=rp, tp=tp: s.copy(
                    X1T[:].rearrange("p (c sl tk) -> p c sl tk", c=EC, sl=8)[:, :, tp, :],
                    PT[rp][:, 0:1024].rearrange("p (c tk) -> p c tk", c=EC)
                ).then_inc(SS, 1))
                x1te[tp] = inc('s')
        return C['s']

    def emit_stage1(h, hook=None, ygate=0):
        ent_w = {0: C['w0'], 1: C['w1']}
        wdma = {}; wtp = {}; wte = {}; wte_lo = {}; mmv = {}; sil = {}; hv = {}; cw = {}
        for oc in range(HC):
            if hook is not None:
                hook(oc)
            ro = oc % 2
            if h == 0:
                if oc >= 2:
                    w('g', 't', wtp[oc - 2])
                em('g', lambda g, oc=oc, ro=ro: g.dma_start(
                    out=WSB[:, ro * 1024:(ro + 1) * 1024],
                    in_=w1[oc * 128:(oc + 1) * 128, :]).then_inc(SGP[ro], 16))
                inc(f'g{ro}', 16)
                em('g', lambda g, oc=oc, ro=ro: g.dma_start(
                    out=WSB[:, 2048 + ro * 1024: 2048 + (ro + 1) * 1024],
                    in_=w2[oc * 128:(oc + 1) * 128, :]).then_inc(SGP[ro], 16))
                wdma[oc] = inc(f'g{ro}', 16)
                w('t', f'g{ro}', wdma[oc])
                if oc >= 1:
                    w('t', 's', wte[oc - 1])
                for c in range(EC):
                    em('t', lambda t, ro=ro, c=c: t.transpose(
                        PT[0][:, c * 128:(c + 1) * 128],
                        WSB[:, ro * 1024 + c * 128: ro * 1024 + (c + 1) * 128], IDB[:]))
                for c in range(EC):
                    em('t', lambda t, ro=ro, c=c: fi(t.transpose(
                        PT[1][:, c * 128:(c + 1) * 128],
                        WSB[:, 2048 + ro * 1024 + c * 128: 2048 + ro * 1024 + (c + 1) * 128],
                        IDB[:]), c == EC - 1))
                wtp[oc] = inc('t')
                w('s', 't', wtp[oc])
                if oc >= 2:
                    w('s', 't', mmv[(oc - 2, 1)])
                    w('s', f'w{ro}', cw[oc - 2])
                em('s', lambda s, ro=ro: s.copy(
                    WTB[:, ro * 2048: ro * 2048 + 1024],
                    PT[0][:, 0:1024]).then_inc(SS, 1))
                wte1 = inc('s')
                em('s', lambda s, ro=ro: s.copy(
                    WTB[:, ro * 2048 + 1024: ro * 2048 + 2048],
                    PT[1][:, 0:1024]).then_inc(SS, 1))
                wte[oc] = inc('s')
                wte_lo[oc] = wte1
                w('s', 's', wte[oc])
                em('s', lambda s, oc=oc, ro=ro: s.dma_start(
                    out=wt12[oc],
                    in_=WTB[:, ro * 2048:(ro + 1) * 2048]).then_inc(SWS[ro], 16))
                cw[oc] = inc(f'w{ro}', 16)
            else:
                if oc >= 2:
                    w('y', 't', mmv[(oc - 2, 1)])
                else:
                    w('y', 't', ygate)
                    w('y', f'w{ro}', ent_w[ro])
                em('y', lambda s_, oc=oc, ro=ro: s_.dma_start(
                    out=WTB[:, ro * 2048:(ro + 1) * 2048],
                    in_=wt12[oc]).then_inc(SR[ro], 16))
                wdma[oc] = inc(f'r{ro}', 16)
            for g_i in range(2):
                s_ = oc * 2 + g_i
                rp = s_ % 2
                if g_i == 0:
                    if h == 0:
                        w('t', 's', wte_lo[oc])
                    else:
                        w('t', f'r{ro}', wdma[oc])
                if s_ >= 2:
                    w('t', 's', sil[s_ - 2])
                    w('t', 'v', hv[s_ - 2])
                for c in range(EC):
                    em('t', lambda t, ro=ro, rp=rp, c=c, g_i=g_i: t.matmul(
                        PP[rp][:, 0:512],
                        WTB[:, ro * 2048 + c * 128: ro * 2048 + (c + 1) * 128],
                        X1T[:, c * 1024 + g_i * 512: c * 1024 + g_i * 512 + 512],
                        start=(c == 0), stop=(c == EC - 1)))
                if g_i == 0 and h == 0:
                    w('t', 's', wte[oc])
                for c in range(EC):
                    em('t', lambda t, ro=ro, rp=rp, c=c, g_i=g_i: fi(t.matmul(
                        PP[rp][:, 512:1024],
                        WTB[:, ro * 2048 + 1024 + c * 128: ro * 2048 + 1024 + (c + 1) * 128],
                        X1T[:, c * 1024 + g_i * 512: c * 1024 + g_i * 512 + 512],
                        start=(c == 0), stop=(c == EC - 1)), c == EC - 1))
                mmv[(oc, g_i)] = inc('t')
                w('s', 't', mmv[(oc, g_i)])
                em('s', lambda s, rp=rp, g_i=g_i, oc=oc: s.activation(
                    TS1[:, g_i * 512:(g_i + 1) * 512], PP[rp][:, 0:512], AF.Silu,
                    bias=0.0, scale=SC1[:, oc:oc + 1]).then_inc(SS, 1))
                sil[s_] = inc('s')
                w('v', 's', sil[s_])
                em('v', lambda v, rp=rp, g_i=g_i, oc=oc: v.scalar_tensor_tensor(
                    HB[:, oc * 1024 + g_i * 512: oc * 1024 + g_i * 512 + 512],
                    PP[rp][:, 512:1024], SC2[:, oc:oc + 1],
                    TS1[:, g_i * 512:(g_i + 1) * 512],
                    ALU.mult, ALU.mult).then_inc(SV, 1))
                hv[s_] = inc('v')
        return wtp.get(HC - 1, C['t'])

    def emit_big_stage(wsrc, scl, src_buf, dst_buf, nout, func, cache, h, ygate=0):
        ent_w = {0: C['w0'], 1: C['w1']}
        wdma = {}; wtp = {}; wte = {}; mmv = {}; gel = {}; cw = {}; rdv = {}
        for oc in range(nout):
            so = oc % 2
            if h == 0:
                for half in range(2):
                    s_slot = (oc * 2 + half) % 2
                    if oc >= 1:
                        w('g', 't', wtp[(oc - 1, 2 * half + 1)])
                    em('g', lambda g, oc=oc, half=half, s_slot=s_slot, wsrc=wsrc:
                       g.dma_start(
                        out=WSB[:, s_slot * 2048:(s_slot + 1) * 2048],
                        in_=wsrc[oc * 128:(oc + 1) * 128,
                                 half * 2048:(half + 1) * 2048]).then_inc(SGP[s_slot], 16))
                    wdma[(oc, half)] = inc(f'g{s_slot}', 16)
            else:
                if oc >= 2:
                    w('y', 't', mmv[(oc - 2, 3)])
                else:
                    w('y', 't', ygate)
                    w('y', f'w{so}', ent_w[so])
                em('y', lambda s_, oc=oc, so=so, cache=cache: s_.dma_start(
                    out=WTB[:, so * H:(so + 1) * H],
                    in_=cache[oc]).then_inc(SR[so], 16))
                rdv[oc] = inc(f'r{so}', 16)

            def emit_tp(b):
                half = b // 2
                s_slot = (oc * 2 + half) % 2
                w('t', f'g{s_slot}', wdma[(oc, half)])
                if b >= 2:
                    w('t', 'v', wte[(oc, b - 2)])
                elif oc >= 1:
                    w('t', 'v', wte[(oc - 1, b + 2)])
                for j in range(8):
                    em('t', lambda t, s_slot=s_slot, b=b, j=j: fi(t.transpose(
                        PT[b % 2][:, j * 128:(j + 1) * 128],
                        WSB[:, s_slot * 2048 + (b % 2) * 1024 + j * 128:
                            s_slot * 2048 + (b % 2) * 1024 + (j + 1) * 128],
                        IDB[:]), j == 7))
                wtp[(oc, b)] = inc('t')
                w('v', 't', wtp[(oc, b)])
                if oc >= 2:
                    w('v', 't', mmv[(oc - 2, b)])
                    if b == 0:
                        w('v', f'w{so}', cw[oc - 2])
                em('v', lambda v, so=so, b=b: v.tensor_copy(
                    WTB[:, so * H + b * 1024: so * H + (b + 1) * 1024],
                    PT[b % 2][:, 0:1024]).then_inc(SV, 1))
                wte[(oc, b)] = inc('v')
                if b == 3:
                    w('s', 'v', wte[(oc, 3)])
                    em('s', lambda s, oc=oc, so=so, cache=cache: s.dma_start(
                        out=cache[oc],
                        in_=WTB[:, so * H:(so + 1) * H]).then_inc(SWS[so], 16))
                    cw[oc] = inc(f'w{so}', 16)

            def emit_mm(b):
                rp = oc % 2
                if h == 0:
                    w('t', 'v', wte[(oc, b)])
                elif b == 0:
                    w('t', f'r{so}', rdv[oc])
                if b == 0 and oc >= 2:
                    w('t', 's', gel[oc - 2])
                for j in range(8):
                    ic = b * 8 + j
                    first = (b == 0 and j == 0)
                    last = (b == 3 and j == 7)
                    for g_i in range(2):
                        em('t', lambda t, rp=rp, so=so, b=b, j=j, ic=ic, g_i=g_i,
                           first=first, last=last, src_buf=src_buf: fi(t.matmul(
                            PP[rp][:, g_i * 512:(g_i + 1) * 512],
                            WTB[:, so * H + b * 1024 + j * 128:
                                so * H + b * 1024 + (j + 1) * 128],
                            src_buf[:, ic * 1024 + g_i * 512:
                                    ic * 1024 + g_i * 512 + 512],
                            start=first, stop=last), j == 7 and g_i == 1))
                mmv[(oc, b)] = inc('t')

            if h == 0:
                emit_tp(0)
                emit_tp(1)
                emit_tp(2)
                emit_mm(0)
                emit_tp(3)
                emit_mm(1)
                emit_mm(2)
                emit_mm(3)
            else:
                for b in range(4):
                    emit_mm(b)
            rp = oc % 2
            w('s', 't', mmv[(oc, 3)])
            em('s', lambda s, rp=rp, oc=oc, scl=scl, dst_buf=dst_buf, func=func:
               s.activation(dst_buf[:, oc * 1024:(oc + 1) * 1024],
                            PP[rp][:, 0:1024], func, bias=0.0,
                            scale=scl[:, oc:oc + 1]).then_inc(SS, 1))
            gel[oc] = inc('s')
        return wtp.get((nout - 1, 3), C['t'])

    def make_final_ln(h, two_buf=False):
        st = {"ftp": {}, "fte": {}, "addv": {}, "xdma": {}, "outst": {}}
        nb = 2 if two_buf else 1
        ysnap = {0: C['y0'], 1: C['y1']}
        zsnap = {0: C['z0'], 1: C['z1']}
        vsnap = C['v']

        def step(tau):
            ftp, fte, addv, xdma, outst = (st["ftp"], st["fte"], st["addv"],
                                           st["xdma"], st["outst"])
            th = 8 * h + tau
            r = tau % 2
            if tau >= 2:
                w('t', 's', fte[tau - 2])
            for c in range(EC):
                em('t', lambda t, r=r, c=c, tau=tau: fi(t.transpose(
                    PT[r][:, c * 128:(c + 1) * 128],
                    GB[:, c * 1024 + tau * 128: c * 1024 + (tau + 1) * 128],
                    IDB[:]), c == EC - 1))
            ftp[tau] = inc('t')
            buf = FT if (not two_buf or tau % 2 == 0) else TS1
            w('s', 't', ftp[tau])
            if tau >= nb:
                w('s', 'v', addv[tau - nb])
            em('s', lambda s, r=r, buf=buf: s.copy(
                buf[:, 0:1024], PT[r][:, 0:1024]).then_inc(SS, 1))
            fte[tau] = inc('s')
            if tau >= 2:
                w('y', 'v', addv[tau - 2])
            else:
                w('y', f'y{r}', ysnap[r])
                w('y', 'v', vsnap)
            em('y', lambda s_, th=th, r=r: s_.dma_start(
                out=X1R[:, r * E:(r + 1) * E],
                in_=x1s[th * 128:(th + 1) * 128, :]).then_inc(SYL[r], 16))
            xdma[tau] = inc(f'y{r}', 16)
            w('v', 's', fte[tau])
            w('v', f'y{r}', xdma[tau])
            if tau >= 2:
                w('v', f'z{r}', outst[tau - 2])
            else:
                w('v', f'z{r}', zsnap[r])
            em('v', lambda v, r=r, buf=buf: v.scalar_tensor_tensor(
                X1F[:, r * E:(r + 1) * E], X1R[:, r * E:(r + 1) * E], 1.0,
                buf[:, 0:1024], ALU.mult, ALU.add,
                accum_out=NM[:, r:r + 1]).then_inc(SV, 1))
            inc('v')
            lnv = emit_ln(r, r, dump=buf)
            addv[tau] = lnv
            w('y', 'v', lnv)
            if tau >= 2:
                w('y', f'z{r}', outst[tau - 2])
            else:
                w('y', f'z{r}', zsnap[r])
            em('y', lambda s_, th=th, r=r: s_.dma_start(
                out=y[th * 128:(th + 1) * 128, :],
                in_=X1F[:, r * E:(r + 1) * E]).then_inc(SYS[r], 16))
            outst[tau] = inc(f'z{r}', 16)

        return step

    replay()
    ctxA.close()

    X1T = EE(nc.sbuf_tensor("X1T", [128, EC * 1024], BF16))
    HB = EE(nc.sbuf_tensor("HB", [128, HC * 1024], BF16))
    GB = EE(nc.sbuf_tensor("GB", [128, HC * 1024], BF16))
    WSB = EE(nc.sbuf_tensor("WSB", [128, H], BF16))
    WTB = EE(nc.sbuf_tensor("WTB", [128, 2 * H], BF16))
    TS1 = EE(nc.sbuf_tensor("TS1", [128, 1024], F32))
    FT = EE(nc.sbuf_tensor("FT", [128, 1024], F32))
    X1R = EE(nc.sbuf_tensor("X1R", [128, 2 * E], F32))
    X1F = EE(nc.sbuf_tensor("X1F", [128, 2 * E], F32))
    X1B = EE(nc.sbuf_tensor("X1B", [128, 2 * E], BF16))
    NM = EE(nc.sbuf_tensor("NM", [128, 2], F32))
    MM = EE(nc.sbuf_tensor("MM", [128, 2], F32))
    VAR = EE(nc.sbuf_tensor("VAR", [128, 2], F32))
    NMR = EE(nc.sbuf_tensor("NMR", [128, 2], F32))
    NM2 = EE(nc.sbuf_tensor("NM2", [128, 2], F32))
    SSQ = EE(nc.sbuf_tensor("SSQ", [128, 2], F32))
    SD = EE(nc.sbuf_tensor("SD", [128, 2], F32))
    RS = EE(nc.sbuf_tensor("RS", [128, 2], F32))
    for h in range(2):
        pre_b3_gate = C['t']
        emit_b3(h)
        light_handoff(pre_b3_gate)
        if h == 0:
            g1_ = emit_stage1(0)
        else:
            fstep0 = make_final_ln(0)
            g1_ = emit_stage1(1, hook=lambda oc: fstep0(oc // 4) if oc % 4 == 0 else None,
                              ygate=pre_b3_gate)
        light_handoff(g1_)
        g2_ = emit_big_stage(h1, SCH1, HB, GB, HC, AF.Gelu, h1t, h, ygate=g1_)
        light_handoff(g2_)
        g3_ = emit_big_stage(h2, SCH2, GB, HB, HC, AF.Gelu, h2t, h, ygate=g2_)
        light_handoff(g3_)
        g4_ = emit_big_stage(w3, SC3, HB, GB, EC, AF.Copy, w3t, h, ygate=g3_)
        if h == 0:
            light_handoff(g4_)
        else:
            barrier()
    fstep1 = make_final_ln(1, two_buf=True)
    for tau in range(8):
        fstep1(tau)
    barrier()

    em('y', lambda s_: s_.wait_ge(SYS[0], C['z0']))
    em('y', lambda s_: s_.wait_ge(SYS[1], C['z1']))
    replay()

    _CACHE['ctx'] = ctx
    return nc


def kernel(**inputs):
    def arr(k):
        return np.ascontiguousarray(np.asarray(inputs[k], dtype=np.float32))

    x = arr("L0_attn_x").reshape(NTOK, E)
    base = {
        "wq": arr("L0_attn_wq"), "wk": arr("L0_attn_wk"),
        "wv": arr("L0_attn_wv"), "wo": arr("L0_attn_wo"),
        "w1": arr("L0_ffn_w1_w"), "w2": arr("L0_ffn_w2_w"),
        "h1": arr("L0_ffn_h1_w"), "h2": arr("L0_ffn_h2_w"),
        "w3": arr("L0_ffn_w3_w"),
        "s1v": arr("L0_ffn_w1_s").reshape(32, 128),
        "s2v": arr("L0_ffn_w2_s").reshape(32, 128),
        "sh1v": arr("L0_ffn_h1_s").reshape(32, 128),
        "sh2v": arr("L0_ffn_h2_s").reshape(32, 128),
        "s3v": arr("L0_ffn_w3_s").reshape(8, 128),
    }
    if 'nc' not in _CACHE:
        _CACHE['nc'] = build()
    nc = _CACHE['nc']
    in_maps = []
    for c in range(8):
        m = dict(base)
        m["x"] = np.roll(x, -c * OWN, axis=0) if c else x
        in_maps.append(m)
    res = run_bass_kernel_spmd(nc, in_maps, core_ids=list(range(8)))
    out = np.concatenate([np.asarray(res.results[c]["y"]) for c in range(8)], axis=0)
    return out.reshape(1, NTOK, E).astype(np.float32)


# version 23
# speedup vs baseline: 1.0369x; 1.0369x over previous
import math
import numpy as np
import ml_dtypes
import concourse.bass as bass
import concourse.mybir as mybir
from concourse.bass_utils import run_bass_kernel_spmd
from contextlib import ExitStack

F32 = mybir.dt.float32
BF16 = mybir.dt.bfloat16
AF = mybir.ActivationFunctionType
ALU = mybir.AluOpType
AX = mybir.AxisListType

NTOK = 16384
E = 1024
EC = E // 128
D = 64
H = 4096
HC = H // 128
OWN = 2048
NT = NTOK // 128
OT = OWN // 128
CTX_SCALE = math.sqrt(3.0 / D)

_CACHE = {}


def build():
    nc = bass.Bass("TRN2", target_bir_lowering=False)
    x = nc.dram_tensor("x", [NTOK, E], F32, kind="ExternalInput").ap()
    wq = nc.dram_tensor("wq", [D, E], F32, kind="ExternalInput").ap()
    wk = nc.dram_tensor("wk", [D, E], F32, kind="ExternalInput").ap()
    wv = nc.dram_tensor("wv", [D, E], F32, kind="ExternalInput").ap()
    wo = nc.dram_tensor("wo", [E, D], F32, kind="ExternalInput").ap()
    w1 = nc.dram_tensor("w1", [H, E], F32, kind="ExternalInput").ap()
    w2 = nc.dram_tensor("w2", [H, E], F32, kind="ExternalInput").ap()
    h1 = nc.dram_tensor("h1", [H, H], F32, kind="ExternalInput").ap()
    h2 = nc.dram_tensor("h2", [H, H], F32, kind="ExternalInput").ap()
    w3 = nc.dram_tensor("w3", [E, H], F32, kind="ExternalInput").ap()
    s1v = nc.dram_tensor("s1v", [32, 128], F32, kind="ExternalInput").ap()
    s2v = nc.dram_tensor("s2v", [32, 128], F32, kind="ExternalInput").ap()
    sh1v = nc.dram_tensor("sh1v", [32, 128], F32, kind="ExternalInput").ap()
    sh2v = nc.dram_tensor("sh2v", [32, 128], F32, kind="ExternalInput").ap()
    s3v = nc.dram_tensor("s3v", [8, 128], F32, kind="ExternalInput").ap()
    y = nc.dram_tensor("y", [OWN, E], F32, kind="ExternalOutput").ap()
    x1s = nc.dram_tensor("x1s", [OWN, E], F32).ap()
    wt12 = nc.dram_tensor("wt12", [HC, 128, 2048], BF16).ap()
    h1t = nc.dram_tensor("h1t", [HC, 128, 4096], BF16).ap()
    h2t = nc.dram_tensor("h2t", [HC, 128, 4096], BF16).ap()
    w3t = nc.dram_tensor("w3t", [EC, 128, 4096], BF16).ap()

    ctx = ExitStack()
    EE = ctx.enter_context
    IDB = EE(nc.sbuf_tensor("IDB", [128, 128], BF16))
    IDF = EE(nc.sbuf_tensor("IDF", [128, 128], F32))
    KVQT = EE(nc.sbuf_tensor("KVQT", [128, EC * 192], BF16))
    WOT = EE(nc.sbuf_tensor("WOT", [D, E], BF16))
    SC1 = EE(nc.sbuf_tensor("SC1", [128, 32], F32))
    SC2 = EE(nc.sbuf_tensor("SC2", [128, 32], F32))
    SCH1 = EE(nc.sbuf_tensor("SCH1", [128, 32], F32))
    SCH2 = EE(nc.sbuf_tensor("SCH2", [128, 32], F32))
    SC3 = EE(nc.sbuf_tensor("SC3", [128, 8], F32))
    CT = EE(nc.sbuf_tensor("CT", [D, OWN], BF16))
    CST = EE(nc.sbuf_tensor("CST", [128, 4], F32))
    ctxA = ExitStack()
    EA = ctxA.enter_context
    WKn = EA(nc.sbuf_tensor("WKn", [D, E], BF16))
    WVn = EA(nc.sbuf_tensor("WVn", [D, E], BF16))
    WQn = EA(nc.sbuf_tensor("WQn", [D, E], BF16))
    WOn = EA(nc.sbuf_tensor("WOn", [128, 512], BF16))
    SCL = [EA(nc.sbuf_tensor(f"SCL{i}", [32, 128], F32)) for i in range(4)]
    SCL3 = EA(nc.sbuf_tensor("SCLW3", [8, 128], F32))
    XB = EA(nc.sbuf_tensor("XB", [128, 8 * E], BF16))
    XT = EA(nc.sbuf_tensor("XT", [128, 8 * E], BF16))
    KVB = EA(nc.sbuf_tensor("KVB", [128, 4 * 192], BF16))
    QN = EA(nc.sbuf_tensor("QN", [128, OT * D], BF16))
    ENF = EA(nc.sbuf_tensor("ENF", [D, D], F32))
    ENB = EA(nc.sbuf_tensor("ENB", [D, D], BF16))
    QT = EA(nc.sbuf_tensor("QT", [D, OWN], BF16))
    PT = [EE(nc.psum_tensor("PT1", [128, 1024], BF16)),
          EE(nc.psum_tensor("PT2", [128, 1024], BF16))]
    PP = [EE(nc.psum_tensor("PA", [128, 1024], F32)),
          EE(nc.psum_tensor("PB", [128, 1024], F32))]
    PEn = EE(nc.psum_tensor("PEn", [D, D], F32))
    PM = EE(nc.psum_tensor("PM", [128, 512], F32))
    SL = EE(nc.semaphore("SL"))
    SGP = [EE(nc.semaphore("SGP0")), EE(nc.semaphore("SGP1")),
           EE(nc.semaphore("SGP2")), EE(nc.semaphore("SGP3"))]
    SYL = [EE(nc.semaphore("SYL0")), EE(nc.semaphore("SYL1"))]
    SYS = [EE(nc.semaphore("SYS0")), EE(nc.semaphore("SYS1"))]
    SWS = [EE(nc.semaphore("SWS0")), EE(nc.semaphore("SWS1"))]
    SUS = [EE(nc.semaphore("SUS0")), EE(nc.semaphore("SUS1"))]
    ST = EE(nc.semaphore("ST"))
    SS = EE(nc.semaphore("SS"))
    SV = EE(nc.semaphore("SV"))
    SP = EE(nc.semaphore("SP"))

    Q = {k: [] for k in "gystv"}
    C = {"l": 0, "g0": 0, "g1": 0, "g2": 0, "g3": 0,
         "y0": 0, "y1": 0, "z0": 0, "z1": 0,
         "w0": 0, "w1": 0, "u0": 0, "u1": 0,
         "s": 0, "t": 0, "v": 0, "p": 0}
    SEM = {"l": SL, "g0": SGP[0], "g1": SGP[1], "g2": SGP[2], "g3": SGP[3],
           "y0": SYL[0], "y1": SYL[1],
           "z0": SYS[0], "z1": SYS[1],
           "w0": SWS[0], "w1": SWS[1], "u0": SUS[0], "u1": SUS[1],
           "s": SS, "t": ST, "v": SV, "p": SP}

    def em(q, f):
        Q[q].append(f)

    def replay():
        with nc.Block() as block:
            @block.gpsimd
            def _(g):
                for f in Q['g']:
                    f(g)

            @block.sync
            def _(s_):
                for f in Q['y']:
                    f(s_)

            @block.tensor
            def _(t):
                for f in Q['t']:
                    f(t)

            @block.scalar
            def _(s):
                for f in Q['s']:
                    f(s)

            @block.vector
            def _(v):
                for f in Q['v']:
                    f(v)
        for k in Q:
            Q[k] = []

    def w(q, src, val):
        if val > 0:
            em(q, lambda e, s=SEM[src], v=val: e.wait_ge(s, v))

    def inc(q, n=1):
        C[q] += n
        return C[q]

    def fi(ins, fin, sem=None):
        if fin:
            ins.then_inc(sem if sem is not None else ST, 1)
        return ins

    QENG = {"l": "g", "g0": "g", "g1": "g", "g2": "g", "g3": "g",
            "y0": "y", "y1": "y",
            "z0": "y", "z1": "y",
            "w0": "s", "w1": "s", "u0": "s", "u1": "s",
            "s": "s", "t": "t", "v": "v", "p": "g"}

    def barrier():
        snap = dict(C)
        for q in "gystv":
            for o in C:
                if QENG[o] != q:
                    w(q, o, snap[o])

    def light_handoff(pe_gate):
        w('t', 's', C['s'])
        w('t', 'v', C['v'])
        w('s', 'w0', C['w0'])
        w('s', 'w1', C['w1'])
        w('v', 'w0', C['w0'])
        w('v', 'w1', C['w1'])
        w('g', 't', pe_gate)
        w('g', 'w0', C['w0'])
        w('g', 'w1', C['w1'])

    def aff_ident(t_):
        em('g', lambda g, t_=t_: g.memset(t_[:], 0.0).then_inc(SP, 1))
        v_ = inc('p', 1)
        w('g', 'p', v_)
        em('g', lambda g, t_=t_: g.affine_select(
            out=t_[:], in_=t_[:], compare_op=ALU.not_equal, fill=1.0, base=0,
            pattern=[[-1, 128]], channel_multiplier=1).then_inc(SP, 1))
        inc('p', 1)

    em('g', lambda g: g.memset(CST[:, 0:1], 1e-5))
    em('g', lambda g: g.memset(CST[:, 1:2], 1.0 / E))
    em('g', lambda g: g.memset(CST[:, 2:3], CTX_SCALE).then_inc(SP, 1))
    inc('p', 1)
    aff_ident(IDB)
    aff_ident(IDF)
    em('g', lambda g: g.dma_start(out=WKn[:], in_=wk).then_inc(SL, 16)); inc('l', 16)
    em('g', lambda g: g.dma_start(out=WVn[:], in_=wv).then_inc(SL, 16)); inc('l', 16)
    em('g', lambda g: g.dma_start(out=WQn[:], in_=wq).then_inc(SL, 16)); inc('l', 16)
    em('g', lambda g: g.dma_start(
        out=WOn[:].rearrange("p (a d) -> p a d", a=8),
        in_=wo.rearrange("(a p) d -> p a d", p=128)).then_inc(SL, 16)); inc('l', 16)
    for i, (dst, src) in enumerate([(SCL[0], s1v), (SCL[1], s2v), (SCL[2], sh1v),
                                    (SCL[3], sh2v), (SCL3, s3v)]):
        em('g', lambda g, d_=dst, s_=src: g.dma_start(out=d_[:], in_=s_).then_inc(SL, 16))
        inc('l', 16)
    g_loads = C['l']

    w('t', 'l', g_loads)
    w('t', 'p', C['p'])
    for wi, (wn, off) in enumerate([(WKn, 0), (WVn, 64), (WQn, 128)]):
        for c in range(EC):
            em('t', lambda t, wn=wn, c=c: fi(t.transpose(
                PT[0][0:128, c * 64:(c + 1) * 64], wn[0:D, c * 128:(c + 1) * 128],
                IDB[0:D, 0:D]), c == EC - 1))
        v_tp = inc('t')
        w('s', 't', v_tp)
        em('s', lambda s, off=off: s.copy(
            KVQT[:].rearrange("p (c t) -> p c t", c=EC)[:, :, off:off + 64],
            PT[0][:, 0:512].rearrange("p (c t) -> p c t", t=64)).then_inc(SS, 1))
        v_ev = inc('s')
        w('t', 's', v_ev)
    for c in range(EC):
        em('t', lambda t, c=c: fi(t.transpose(
            PT[1][0:D, c * 128:(c + 1) * 128], WOn[:, c * 64:(c + 1) * 64],
            IDB[:]), c == EC - 1))
    v_wot = inc('t')
    w('s', 't', v_wot)
    em('s', lambda s: s.copy(WOT[:], PT[1][0:D, 0:1024]).then_inc(SS, 1)); inc('s')
    for i in range(4):
        em('t', lambda t, i=i: t.transpose(
            PM[0:128, i * 32:(i + 1) * 32], SCL[i][0:32, :], IDF[0:32, 0:32]))
    em('t', lambda t: fi(t.transpose(PM[0:128, 4 * 32:4 * 32 + 8], SCL3[0:8, :],
                                     IDF[0:8, 0:8]), True))
    v_sct = inc('t')
    w('s', 't', v_sct)
    for i, dst in enumerate([SC1, SC2, SCH1, SCH2]):
        em('s', lambda s, d_=dst, i=i: s.copy(d_[:], PM[:, i * 32:(i + 1) * 32]))
    em('s', lambda s: s.copy(SC3[:], PM[:, 128:136]).then_inc(SS, 1)); inc('s')
    barrier()

    xtp = {}; xte = {}; kvq = {}; kve = {}; emm = {}
    xb_dma = {}
    KVQP = [PP[0][:, 0:192], PP[0][:, 512:704],
            PP[1][:, 0:192], PP[1][:, 512:704]]
    for t_i in range(NT + 4):
        if t_i < NT:
            r8 = t_i % 8
            rs = t_i % 4
            r2 = t_i % 2
            if t_i >= 8:
                w('g', 't', xtp[t_i - 8])
            if t_i >= 4:
                w('g', f'g{rs}', xb_dma[t_i - 4])
            em('g', lambda g, t_i=t_i, r8=r8, rs=rs: g.dma_start(
                out=XB[:, r8 * E:(r8 + 1) * E],
                in_=x[t_i * 128:(t_i + 1) * 128, :]).then_inc(SGP[rs], 16))
            xb_dma[t_i] = inc(f'g{rs}', 16)
            w('t', f'g{rs}', xb_dma[t_i])
            if t_i >= 2:
                w('t', 's', xte[t_i - 2])
            for c in range(EC):
                em('t', lambda t, r2=r2, r8=r8, c=c: fi(t.transpose(
                    PT[r2][:, c * 128:(c + 1) * 128],
                    XB[:, r8 * E + c * 128: r8 * E + (c + 1) * 128], IDB[:]),
                    c == EC - 1))
            xtp[t_i] = inc('t')
            w('s', 't', xtp[t_i])
            if t_i >= 8:
                w('s', 't', kvq[t_i - 8])
            em('s', lambda s, r2=r2, r8=r8: s.copy(
                XT[:, r8 * E:(r8 + 1) * E], PT[r2][:, 0:1024]).then_inc(SS, 1))
            xte[t_i] = inc('s')
        tj = t_i - 2
        if 0 <= tj < NT:
            rj = tj % 4
            rx = tj % 8
            W_ = 192 if tj < OT else 128
            w('t', 's', xte[tj])
            if tj >= 4:
                w('t', 'v', kve[tj - 4])
            for c in range(EC):
                em('t', lambda t, rj=rj, rx=rx, c=c, W_=W_: fi(t.matmul(
                    KVQP[rj][:, 0:W_],
                    XT[:, rx * E + c * 128: rx * E + (c + 1) * 128],
                    KVQT[:, c * 192: c * 192 + W_],
                    start=(c == 0), stop=(c == EC - 1)), c == EC - 1))
            kvq[tj] = inc('t')
            w('v', 't', kvq[tj])
            if tj >= 4:
                w('v', 't', emm[tj - 4])
            own_ = tj < OT
            em('v', lambda v, rj=rj, W_=W_, own_=own_: fi(v.tensor_copy(
                KVB[:, rj * 192: rj * 192 + W_], KVQP[rj][:, 0:W_]),
                not own_, SV))
            if own_:
                em('v', lambda v, tj=tj, rj=rj: fi(v.tensor_copy(
                    QN[:, tj * D:(tj + 1) * D], KVQP[rj][:, 128:192]), True, SV))
            kve[tj] = inc('v')
        tk = t_i - 4
        if 0 <= tk < NT:
            rk = tk % 4
            w('t', 'v', kve[tk])
            em('t', lambda t, rk=rk, tk=tk: t.matmul(
                PEn[:], KVB[:, rk * 192: rk * 192 + 64],
                KVB[:, rk * 192 + 64: rk * 192 + 128],
                start=(tk == 0), stop=(tk == NT - 1)).then_inc(ST, 1))
            emm[tk] = inc('t')
    barrier()

    em('s', lambda s: s.activation(ENF[:], PEn[:], AF.Copy, bias=0.0,
                                   scale=CST[0:D, 2:3]).then_inc(SS, 1))
    v_enf = inc('s')
    w('v', 's', v_enf)
    em('v', lambda v: v.tensor_copy(ENB[:], ENF[:]).then_inc(SV, 1)); v_enb = inc('v')
    for b in range(2):
        for k in range(8):
            ki = b * 8 + k
            em('t', lambda t, b=b, k=k, ki=ki: fi(t.transpose(
                PT[b][0:D, k * 128:(k + 1) * 128], QN[:, ki * D:(ki + 1) * D],
                IDB[:]), k == 7))
        v_qtp = inc('t')
        w('s', 't', v_qtp)
        em('s', lambda s, b=b: s.copy(
            QT[:, b * 1024:(b + 1) * 1024], PT[b][0:D, 0:1024]).then_inc(SS, 1))
        inc('s')
    v_qte = C['s']
    w('t', 'v', v_enb)
    w('t', 's', v_qte)
    for g_i in range(4):
        em('t', lambda t, g_i=g_i: t.matmul(
            PM[0:D, 0:512], ENB[:], QT[:, g_i * 512:(g_i + 1) * 512],
            start=True, stop=True).then_inc(ST, 1))
        v_ctm = inc('t')
        w('s', 't', v_ctm)
        em('s', lambda s, g_i=g_i: s.copy(
            CT[:, g_i * 512:(g_i + 1) * 512], PM[0:D, 0:512]).then_inc(SS, 1))
        v_cte = inc('s')
        w('t', 's', v_cte)
    barrier()

    def emit_ln(r, slot, dump=None):
        dump = TS1 if dump is None else dump
        w('s', 'v', C['v'])
        em('s', lambda s, slot=slot: s.activation(
            NM2[:, slot:slot + 1], NM[:, slot:slot + 1], AF.Copy, bias=0.0,
            scale=CST[:, 1:2]).then_inc(SS, 1))
        inc('s')
        em('s', lambda s, r=r, slot=slot, dump=dump: s.activation(
            dump[:, 0:1024], X1F[:, r * E:(r + 1) * E], AF.Square,
            bias=0.0, scale=1.0, accum_out=SSQ[:, slot:slot + 1]).then_inc(SS, 1))
        sq_ = inc('s')
        w('v', 's', sq_)
        em('v', lambda v, slot=slot: v.tensor_mul(
            MM[:, slot:slot + 1], NM2[:, slot:slot + 1],
            NM2[:, slot:slot + 1]).then_inc(SV, 1))
        vmm = inc('v')
        w('v', 'v', vmm)
        em('v', lambda v, slot=slot: v.scalar_tensor_tensor(
            VAR[:, slot:slot + 1], SSQ[:, slot:slot + 1], CST[:, 1:2],
            MM[:, slot:slot + 1], ALU.mult, ALU.subtract).then_inc(SV, 1))
        vv = inc('v')
        w('s', 'v', vv)
        em('s', lambda s, slot=slot: s.activation(
            SD[:, slot:slot + 1], VAR[:, slot:slot + 1], AF.Sqrt,
            bias=CST[:, 0:1], scale=1.0).then_inc(SS, 1))
        s2_ = inc('s')
        w('v', 's', s2_)
        em('v', lambda v, slot=slot: v.reciprocal(
            RS[:, slot:slot + 1], SD[:, slot:slot + 1]).then_inc(SV, 1))
        vr_ = inc('v')
        w('v', 'v', vr_)
        em('v', lambda v, slot=slot: v.tensor_scalar(
            NMR[:, slot:slot + 1], RS[:, slot:slot + 1], NM2[:, slot:slot + 1],
            -1.0, ALU.mult, ALU.mult).then_inc(SV, 1))
        vn_ = inc('v')
        w('v', 'v', vn_)
        em('v', lambda v, r=r, slot=slot: v.tensor_scalar(
            X1F[:, r * E:(r + 1) * E], X1F[:, r * E:(r + 1) * E],
            RS[:, slot:slot + 1], NMR[:, slot:slot + 1],
            ALU.mult, ALU.add).then_inc(SV, 1))
        return inc('v')

    def emit_b3(h):
        attn = {}; addv = {}; x1tp = {}; x1te = {}; xdma = {}; x1st = {}
        ysnap = {0: C['y0'], 1: C['y1']}
        zsnap = {0: C['z0'], 1: C['z1']}
        vsnap = C['v']
        for ti in range(OT // 2 + 1):
            tau = ti
            th = 8 * h + tau
            r = tau % 2
            if tau < 8:
                if tau >= 2:
                    w('y', 'v', addv[tau - 2])
                else:
                    w('y', f'y{r}', ysnap[r])
                    w('y', 'v', vsnap)
                em('y', lambda s_, th=th, r=r: s_.dma_start(
                    out=X1R[:, r * E:(r + 1) * E],
                    in_=x[th * 128:(th + 1) * 128, :]).then_inc(SYL[r], 16))
                xdma[tau] = inc(f'y{r}', 16)
                if tau >= 2:
                    w('t', 'v', addv[tau - 2])
                for g_i in range(2):
                    em('t', lambda t, th=th, r=r, g_i=g_i: fi(t.matmul(
                        PP[r][:, g_i * 512:(g_i + 1) * 512],
                        CT[:, th * 128:(th + 1) * 128],
                        WOT[:, g_i * 512:(g_i + 1) * 512],
                        start=True, stop=True), g_i == 1))
                attn[tau] = inc('t')
                w('v', 't', attn[tau])
                w('v', f'y{r}', xdma[tau])
                if tau >= 2:
                    w('v', f'z{r}', x1st[tau - 2])
                else:
                    w('v', f'z{r}', zsnap[r])
                em('v', lambda v, r=r: v.scalar_tensor_tensor(
                    X1F[:, r * E:(r + 1) * E], PP[r][:, 0:1024], 1.0,
                    X1R[:, r * E:(r + 1) * E], ALU.mult, ALU.add,
                    accum_out=NM[:, r:r + 1]).then_inc(SV, 1))
                inc('v')
                lnv = emit_ln(r, r)
                w('y', 'v', lnv)
                if tau >= 2:
                    w('y', f'z{r}', x1st[tau - 2])
                else:
                    w('y', f'z{r}', zsnap[r])
                em('y', lambda s_, th=th, r=r: s_.dma_start(
                    out=x1s[th * 128:(th + 1) * 128, :],
                    in_=X1F[:, r * E:(r + 1) * E]).then_inc(SYS[r], 16))
                x1st[tau] = inc(f'z{r}', 16)
                if tau >= 2:
                    w('v', 't', x1tp[tau - 2])
                w('v', 'v', lnv)
                em('v', lambda v, r=r: v.tensor_copy(
                    X1B[:, r * E:(r + 1) * E],
                    X1F[:, r * E:(r + 1) * E]).then_inc(SV, 1))
                addv[tau] = inc('v')
            tp = ti - 1
            if 0 <= tp < 8:
                rp = tp % 2
                w('t', 'v', addv[tp])
                if tp >= 2:
                    w('t', 's', x1te[tp - 2])
                for c in range(EC):
                    em('t', lambda t, rp=rp, c=c: fi(t.transpose(
                        PT[rp][:, c * 128:(c + 1) * 128],
                        X1B[:, rp * E + c * 128: rp * E + (c + 1) * 128],
                        IDB[:]), c == EC - 1))
                x1tp[tp] = inc('t')
                w('s', 't', x1tp[tp])
                em('s', lambda s, rp=rp, tp=tp: s.copy(
                    X1T[:].rearrange("p (c sl tk) -> p c sl tk", c=EC, sl=8)[:, :, tp, :],
                    PT[rp][:, 0:1024].rearrange("p (c tk) -> p c tk", c=EC)
                ).then_inc(SS, 1))
                x1te[tp] = inc('s')
        return C['s']

    def emit_stage1(h, hook=None):
        wdma = {}; wtp = {}; wte = {}; wte_lo = {}; mmv = {}; sil = {}; hv = {}; cw = {}
        for oc in range(HC):
            if hook is not None:
                hook(oc)
            ro = oc % 2
            if h == 0:
                if oc >= 2:
                    w('g', 't', wtp[oc - 2])
                em('g', lambda g, oc=oc, ro=ro: g.dma_start(
                    out=WSB[:, ro * 1024:(ro + 1) * 1024],
                    in_=w1[oc * 128:(oc + 1) * 128, :]).then_inc(SGP[ro], 16))
                inc(f'g{ro}', 16)
                em('g', lambda g, oc=oc, ro=ro: g.dma_start(
                    out=WSB[:, 2048 + ro * 1024: 2048 + (ro + 1) * 1024],
                    in_=w2[oc * 128:(oc + 1) * 128, :]).then_inc(SGP[ro], 16))
                wdma[oc] = inc(f'g{ro}', 16)
                w('t', f'g{ro}', wdma[oc])
                if oc >= 1:
                    w('t', 's', wte[oc - 1])
                for c in range(EC):
                    em('t', lambda t, ro=ro, c=c: t.transpose(
                        PT[0][:, c * 128:(c + 1) * 128],
                        WSB[:, ro * 1024 + c * 128: ro * 1024 + (c + 1) * 128], IDB[:]))
                for c in range(EC):
                    em('t', lambda t, ro=ro, c=c: fi(t.transpose(
                        PT[1][:, c * 128:(c + 1) * 128],
                        WSB[:, 2048 + ro * 1024 + c * 128: 2048 + ro * 1024 + (c + 1) * 128],
                        IDB[:]), c == EC - 1))
                wtp[oc] = inc('t')
                w('s', 't', wtp[oc])
                if oc >= 2:
                    w('s', 't', mmv[(oc - 2, 1)])
                    w('s', f'w{ro}', cw[oc - 2])
                em('s', lambda s, ro=ro: s.copy(
                    WTB[:, ro * 2048: ro * 2048 + 1024],
                    PT[0][:, 0:1024]).then_inc(SS, 1))
                wte1 = inc('s')
                em('s', lambda s, ro=ro: s.copy(
                    WTB[:, ro * 2048 + 1024: ro * 2048 + 2048],
                    PT[1][:, 0:1024]).then_inc(SS, 1))
                wte[oc] = inc('s')
                wte_lo[oc] = wte1
                w('s', 's', wte[oc])
                em('s', lambda s, oc=oc, ro=ro: s.dma_start(
                    out=wt12[oc],
                    in_=WTB[:, ro * 2048:(ro + 1) * 2048]).then_inc(SWS[ro], 16))
                cw[oc] = inc(f'w{ro}', 16)
            else:
                if oc >= 2:
                    w('g', 't', mmv[(oc - 2, 1)])
                em('g', lambda g, oc=oc, ro=ro: g.dma_start(
                    out=WTB[:, ro * 2048:(ro + 1) * 2048],
                    in_=wt12[oc]).then_inc(SGP[ro], 16))
                wdma[oc] = inc(f'g{ro}', 16)
            for g_i in range(2):
                s_ = oc * 2 + g_i
                rp = s_ % 2
                if g_i == 0:
                    if h == 0:
                        w('t', 's', wte_lo[oc])
                    else:
                        w('t', f'g{ro}', wdma[oc])
                if s_ >= 2:
                    w('t', 's', sil[s_ - 2])
                    w('t', 'v', hv[s_ - 2])
                for c in range(EC):
                    em('t', lambda t, ro=ro, rp=rp, c=c, g_i=g_i: t.matmul(
                        PP[rp][:, 0:512],
                        WTB[:, ro * 2048 + c * 128: ro * 2048 + (c + 1) * 128],
                        X1T[:, c * 1024 + g_i * 512: c * 1024 + g_i * 512 + 512],
                        start=(c == 0), stop=(c == EC - 1)))
                if g_i == 0 and h == 0:
                    w('t', 's', wte[oc])
                for c in range(EC):
                    em('t', lambda t, ro=ro, rp=rp, c=c, g_i=g_i: fi(t.matmul(
                        PP[rp][:, 512:1024],
                        WTB[:, ro * 2048 + 1024 + c * 128: ro * 2048 + 1024 + (c + 1) * 128],
                        X1T[:, c * 1024 + g_i * 512: c * 1024 + g_i * 512 + 512],
                        start=(c == 0), stop=(c == EC - 1)), c == EC - 1))
                mmv[(oc, g_i)] = inc('t')
                w('s', 't', mmv[(oc, g_i)])
                em('s', lambda s, rp=rp, g_i=g_i, oc=oc: s.activation(
                    TS1[:, g_i * 512:(g_i + 1) * 512], PP[rp][:, 0:512], AF.Silu,
                    bias=0.0, scale=SC1[:, oc:oc + 1]).then_inc(SS, 1))
                sil[s_] = inc('s')
                w('v', 's', sil[s_])
                em('v', lambda v, rp=rp, g_i=g_i, oc=oc: v.scalar_tensor_tensor(
                    HB[:, oc * 1024 + g_i * 512: oc * 1024 + g_i * 512 + 512],
                    PP[rp][:, 512:1024], SC2[:, oc:oc + 1],
                    TS1[:, g_i * 512:(g_i + 1) * 512],
                    ALU.mult, ALU.mult).then_inc(SV, 1))
                hv[s_] = inc('v')
        return wtp.get(HC - 1, C['t'])

    def emit_big_stage(wsrc, scl, src_buf, dst_buf, nout, func, cache, h):
        wdma = {}; wtp = {}; wte = {}; mmv = {}; gel = {}; cw = {}; rdv = {}
        for oc in range(nout):
            so = oc % 2
            if h == 0:
                for half in range(2):
                    s_slot = (oc * 2 + half) % 2
                    if oc >= 1:
                        w('g', 't', wtp[(oc - 1, 2 * half + 1)])
                    em('g', lambda g, oc=oc, half=half, s_slot=s_slot, wsrc=wsrc:
                       g.dma_start(
                        out=WSB[:, s_slot * 2048:(s_slot + 1) * 2048],
                        in_=wsrc[oc * 128:(oc + 1) * 128,
                                 half * 2048:(half + 1) * 2048]).then_inc(SGP[s_slot], 16))
                    wdma[(oc, half)] = inc(f'g{s_slot}', 16)
            else:
                if oc >= 2:
                    w('g', 't', mmv[(oc - 2, 3)])
                em('g', lambda g, oc=oc, so=so, cache=cache: g.dma_start(
                    out=WTB[:, so * H:(so + 1) * H],
                    in_=cache[oc]).then_inc(SGP[so], 16))
                rdv[oc] = inc(f'g{so}', 16)

            def emit_tp(b):
                half = b // 2
                s_slot = (oc * 2 + half) % 2
                w('t', f'g{s_slot}', wdma[(oc, half)])
                if b >= 2:
                    w('t', 'v', wte[(oc, b - 2)])
                elif oc >= 1:
                    w('t', 'v', wte[(oc - 1, b + 2)])
                for j in range(8):
                    em('t', lambda t, s_slot=s_slot, b=b, j=j: fi(t.transpose(
                        PT[b % 2][:, j * 128:(j + 1) * 128],
                        WSB[:, s_slot * 2048 + (b % 2) * 1024 + j * 128:
                            s_slot * 2048 + (b % 2) * 1024 + (j + 1) * 128],
                        IDB[:]), j == 7))
                wtp[(oc, b)] = inc('t')
                w('v', 't', wtp[(oc, b)])
                if oc >= 2:
                    w('v', 't', mmv[(oc - 2, b)])
                    if b == 0:
                        w('v', f'w{so}', cw[oc - 2])
                em('v', lambda v, so=so, b=b: v.tensor_copy(
                    WTB[:, so * H + b * 1024: so * H + (b + 1) * 1024],
                    PT[b % 2][:, 0:1024]).then_inc(SV, 1))
                wte[(oc, b)] = inc('v')
                if b == 3:
                    w('s', 'v', wte[(oc, 3)])
                    em('s', lambda s, oc=oc, so=so, cache=cache: s.dma_start(
                        out=cache[oc],
                        in_=WTB[:, so * H:(so + 1) * H]).then_inc(SWS[so], 16))
                    cw[oc] = inc(f'w{so}', 16)

            def emit_mm(b):
                rp = oc % 2
                if h == 0:
                    w('t', 'v', wte[(oc, b)])
                elif b == 0:
                    w('t', f'g{so}', rdv[oc])
                if b == 0 and oc >= 2:
                    w('t', 's', gel[oc - 2])
                for j in range(8):
                    ic = b * 8 + j
                    first = (b == 0 and j == 0)
                    last = (b == 3 and j == 7)
                    for g_i in range(2):
                        em('t', lambda t, rp=rp, so=so, b=b, j=j, ic=ic, g_i=g_i,
                           first=first, last=last, src_buf=src_buf: fi(t.matmul(
                            PP[rp][:, g_i * 512:(g_i + 1) * 512],
                            WTB[:, so * H + b * 1024 + j * 128:
                                so * H + b * 1024 + (j + 1) * 128],
                            src_buf[:, ic * 1024 + g_i * 512:
                                    ic * 1024 + g_i * 512 + 512],
                            start=first, stop=last), j == 7 and g_i == 1))
                mmv[(oc, b)] = inc('t')

            if h == 0:
                emit_tp(0)
                emit_tp(1)
                emit_tp(2)
                emit_mm(0)
                emit_tp(3)
                emit_mm(1)
                emit_mm(2)
                emit_mm(3)
            else:
                for b in range(4):
                    emit_mm(b)
            rp = oc % 2
            w('s', 't', mmv[(oc, 3)])
            em('s', lambda s, rp=rp, oc=oc, scl=scl, dst_buf=dst_buf, func=func:
               s.activation(dst_buf[:, oc * 1024:(oc + 1) * 1024],
                            PP[rp][:, 0:1024], func, bias=0.0,
                            scale=scl[:, oc:oc + 1]).then_inc(SS, 1))
            gel[oc] = inc('s')
        return wtp.get((nout - 1, 3), C['t'])

    def make_final_ln(h, two_buf=False):
        st = {"ftp": {}, "fte": {}, "addv": {}, "xdma": {}, "outst": {}}
        nb = 2 if two_buf else 1
        ysnap = {0: C['y0'], 1: C['y1']}
        zsnap = {0: C['z0'], 1: C['z1']}
        vsnap = C['v']

        def step(tau):
            ftp, fte, addv, xdma, outst = (st["ftp"], st["fte"], st["addv"],
                                           st["xdma"], st["outst"])
            th = 8 * h + tau
            r = tau % 2
            if tau >= 2:
                w('t', 's', fte[tau - 2])
            for c in range(EC):
                em('t', lambda t, r=r, c=c, tau=tau: fi(t.transpose(
                    PT[r][:, c * 128:(c + 1) * 128],
                    GB[:, c * 1024 + tau * 128: c * 1024 + (tau + 1) * 128],
                    IDB[:]), c == EC - 1))
            ftp[tau] = inc('t')
            buf = FT if (not two_buf or tau % 2 == 0) else TS1
            w('s', 't', ftp[tau])
            if tau >= nb:
                w('s', 'v', addv[tau - nb])
            em('s', lambda s, r=r, buf=buf: s.copy(
                buf[:, 0:1024], PT[r][:, 0:1024]).then_inc(SS, 1))
            fte[tau] = inc('s')
            if tau >= 2:
                w('y', 'v', addv[tau - 2])
            else:
                w('y', f'y{r}', ysnap[r])
                w('y', 'v', vsnap)
            em('y', lambda s_, th=th, r=r: s_.dma_start(
                out=X1R[:, r * E:(r + 1) * E],
                in_=x1s[th * 128:(th + 1) * 128, :]).then_inc(SYL[r], 16))
            xdma[tau] = inc(f'y{r}', 16)
            w('v', 's', fte[tau])
            w('v', f'y{r}', xdma[tau])
            if tau >= 2:
                w('v', f'z{r}', outst[tau - 2])
            else:
                w('v', f'z{r}', zsnap[r])
            em('v', lambda v, r=r, buf=buf: v.scalar_tensor_tensor(
                X1F[:, r * E:(r + 1) * E], X1R[:, r * E:(r + 1) * E], 1.0,
                buf[:, 0:1024], ALU.mult, ALU.add,
                accum_out=NM[:, r:r + 1]).then_inc(SV, 1))
            inc('v')
            lnv = emit_ln(r, r, dump=buf)
            addv[tau] = lnv
            w('y', 'v', lnv)
            if tau >= 2:
                w('y', f'z{r}', outst[tau - 2])
            else:
                w('y', f'z{r}', zsnap[r])
            em('y', lambda s_, th=th, r=r: s_.dma_start(
                out=y[th * 128:(th + 1) * 128, :],
                in_=X1F[:, r * E:(r + 1) * E]).then_inc(SYS[r], 16))
            outst[tau] = inc(f'z{r}', 16)

        return step

    replay()
    ctxA.close()

    X1T = EE(nc.sbuf_tensor("X1T", [128, EC * 1024], BF16))
    HB = EE(nc.sbuf_tensor("HB", [128, HC * 1024], BF16))
    GB = EE(nc.sbuf_tensor("GB", [128, HC * 1024], BF16))
    WSB = EE(nc.sbuf_tensor("WSB", [128, H], BF16))
    WTB = EE(nc.sbuf_tensor("WTB", [128, 2 * H], BF16))
    TS1 = EE(nc.sbuf_tensor("TS1", [128, 1024], F32))
    FT = EE(nc.sbuf_tensor("FT", [128, 1024], F32))
    X1R = EE(nc.sbuf_tensor("X1R", [128, 2 * E], F32))
    X1F = EE(nc.sbuf_tensor("X1F", [128, 2 * E], F32))
    X1B = EE(nc.sbuf_tensor("X1B", [128, 2 * E], BF16))
    NM = EE(nc.sbuf_tensor("NM", [128, 2], F32))
    MM = EE(nc.sbuf_tensor("MM", [128, 2], F32))
    VAR = EE(nc.sbuf_tensor("VAR", [128, 2], F32))
    NMR = EE(nc.sbuf_tensor("NMR", [128, 2], F32))
    NM2 = EE(nc.sbuf_tensor("NM2", [128, 2], F32))
    SSQ = EE(nc.sbuf_tensor("SSQ", [128, 2], F32))
    SD = EE(nc.sbuf_tensor("SD", [128, 2], F32))
    RS = EE(nc.sbuf_tensor("RS", [128, 2], F32))
    for h in range(2):
        pre_b3_gate = C['t']
        emit_b3(h)
        light_handoff(pre_b3_gate)
        if h == 0:
            g1_ = emit_stage1(0)
        else:
            fstep0 = make_final_ln(0)
            g1_ = emit_stage1(1, hook=lambda oc: fstep0(oc // 4) if oc % 4 == 0 else None)
        light_handoff(g1_)
        g2_ = emit_big_stage(h1, SCH1, HB, GB, HC, AF.Gelu, h1t, h)
        light_handoff(g2_)
        g3_ = emit_big_stage(h2, SCH2, GB, HB, HC, AF.Gelu, h2t, h)
        light_handoff(g3_)
        g4_ = emit_big_stage(w3, SC3, HB, GB, EC, AF.Copy, w3t, h)
        if h == 0:
            light_handoff(g4_)
        else:
            barrier()
    fstep1 = make_final_ln(1, two_buf=True)
    for tau in range(8):
        fstep1(tau)
    barrier()

    em('y', lambda s_: s_.wait_ge(SYS[0], C['z0']))
    em('y', lambda s_: s_.wait_ge(SYS[1], C['z1']))
    replay()

    _CACHE['ctx'] = ctx
    return nc


def kernel(**inputs):
    def arr(k):
        return np.ascontiguousarray(np.asarray(inputs[k], dtype=np.float32))

    x = arr("L0_attn_x").reshape(NTOK, E)
    base = {
        "wq": arr("L0_attn_wq"), "wk": arr("L0_attn_wk"),
        "wv": arr("L0_attn_wv"), "wo": arr("L0_attn_wo"),
        "w1": arr("L0_ffn_w1_w"), "w2": arr("L0_ffn_w2_w"),
        "h1": arr("L0_ffn_h1_w"), "h2": arr("L0_ffn_h2_w"),
        "w3": arr("L0_ffn_w3_w"),
        "s1v": arr("L0_ffn_w1_s").reshape(32, 128),
        "s2v": arr("L0_ffn_w2_s").reshape(32, 128),
        "sh1v": arr("L0_ffn_h1_s").reshape(32, 128),
        "sh2v": arr("L0_ffn_h2_s").reshape(32, 128),
        "s3v": arr("L0_ffn_w3_s").reshape(8, 128),
    }
    if 'nc' not in _CACHE:
        _CACHE['nc'] = build()
    nc = _CACHE['nc']
    in_maps = []
    for c in range(8):
        m = dict(base)
        m["x"] = np.roll(x, -c * OWN, axis=0) if c else x
        in_maps.append(m)
    res = run_bass_kernel_spmd(nc, in_maps, core_ids=list(range(8)))
    out = np.concatenate([np.asarray(res.results[c]["y"]) for c in range(8)], axis=0)
    return out.reshape(1, NTOK, E).astype(np.float32)


# version 24 (final)
# speedup vs baseline: 1.0590x; 1.0213x over previous
import math
import numpy as np
import ml_dtypes
import concourse.bass as bass
import concourse.mybir as mybir
from concourse.bass_utils import run_bass_kernel_spmd
from contextlib import ExitStack

F32 = mybir.dt.float32
BF16 = mybir.dt.bfloat16
AF = mybir.ActivationFunctionType
ALU = mybir.AluOpType
AX = mybir.AxisListType

NTOK = 16384
E = 1024
EC = E // 128
D = 64
H = 4096
HC = H // 128
OWN = 2048
NT = NTOK // 128
OT = OWN // 128
CTX_SCALE = math.sqrt(3.0 / D)

_CACHE = {}


def build():
    nc = bass.Bass("TRN2", target_bir_lowering=False)
    x = nc.dram_tensor("x", [NTOK, E], F32, kind="ExternalInput").ap()
    wq = nc.dram_tensor("wq", [D, E], F32, kind="ExternalInput").ap()
    wk = nc.dram_tensor("wk", [D, E], F32, kind="ExternalInput").ap()
    wv = nc.dram_tensor("wv", [D, E], F32, kind="ExternalInput").ap()
    wo = nc.dram_tensor("wo", [E, D], F32, kind="ExternalInput").ap()
    w1 = nc.dram_tensor("w1", [H, E], F32, kind="ExternalInput").ap()
    w2 = nc.dram_tensor("w2", [H, E], F32, kind="ExternalInput").ap()
    h1 = nc.dram_tensor("h1", [H, H], F32, kind="ExternalInput").ap()
    h2 = nc.dram_tensor("h2", [H, H], F32, kind="ExternalInput").ap()
    w3 = nc.dram_tensor("w3", [E, H], F32, kind="ExternalInput").ap()
    s1v = nc.dram_tensor("s1v", [32, 128], F32, kind="ExternalInput").ap()
    s2v = nc.dram_tensor("s2v", [32, 128], F32, kind="ExternalInput").ap()
    sh1v = nc.dram_tensor("sh1v", [32, 128], F32, kind="ExternalInput").ap()
    sh2v = nc.dram_tensor("sh2v", [32, 128], F32, kind="ExternalInput").ap()
    s3v = nc.dram_tensor("s3v", [8, 128], F32, kind="ExternalInput").ap()
    y = nc.dram_tensor("y", [OWN, E], F32, kind="ExternalOutput").ap()
    x1s = nc.dram_tensor("x1s", [OWN, E], F32).ap()
    wt12 = nc.dram_tensor("wt12", [HC, 128, 2048], BF16).ap()
    h1t = nc.dram_tensor("h1t", [HC, 128, 4096], BF16).ap()
    h2t = nc.dram_tensor("h2t", [HC, 128, 4096], BF16).ap()
    w3t = nc.dram_tensor("w3t", [EC, 128, 4096], BF16).ap()

    ctx = ExitStack()
    EE = ctx.enter_context
    IDB = EE(nc.sbuf_tensor("IDB", [128, 128], BF16))
    IDF = EE(nc.sbuf_tensor("IDF", [128, 128], F32))
    KVQT = EE(nc.sbuf_tensor("KVQT", [128, EC * 192], BF16))
    WOT = EE(nc.sbuf_tensor("WOT", [D, E], BF16))
    SC1 = EE(nc.sbuf_tensor("SC1", [128, 32], F32))
    SC2 = EE(nc.sbuf_tensor("SC2", [128, 32], F32))
    SCH1 = EE(nc.sbuf_tensor("SCH1", [128, 32], F32))
    SCH2 = EE(nc.sbuf_tensor("SCH2", [128, 32], F32))
    SC3 = EE(nc.sbuf_tensor("SC3", [128, 8], F32))
    CT = EE(nc.sbuf_tensor("CT", [D, OWN], BF16))
    CST = EE(nc.sbuf_tensor("CST", [128, 4], F32))
    ctxA = ExitStack()
    EA = ctxA.enter_context
    WKn = EA(nc.sbuf_tensor("WKn", [D, E], BF16))
    WVn = EA(nc.sbuf_tensor("WVn", [D, E], BF16))
    WQn = EA(nc.sbuf_tensor("WQn", [D, E], BF16))
    WOn = EA(nc.sbuf_tensor("WOn", [128, 512], BF16))
    SCL = [EA(nc.sbuf_tensor(f"SCL{i}", [32, 128], F32)) for i in range(4)]
    SCL3 = EA(nc.sbuf_tensor("SCLW3", [8, 128], F32))
    XB = EA(nc.sbuf_tensor("XB", [128, 8 * E], BF16))
    XT = EA(nc.sbuf_tensor("XT", [128, 8 * E], BF16))
    KVB = EA(nc.sbuf_tensor("KVB", [128, 4 * 192], BF16))
    QN = EA(nc.sbuf_tensor("QN", [128, OT * D], BF16))
    ENF = EA(nc.sbuf_tensor("ENF", [D, D], F32))
    ENB = EA(nc.sbuf_tensor("ENB", [D, D], BF16))
    QT = EA(nc.sbuf_tensor("QT", [D, OWN], BF16))
    PT = [EE(nc.psum_tensor("PT1", [128, 1024], BF16)),
          EE(nc.psum_tensor("PT2", [128, 1024], BF16))]
    PP = [EE(nc.psum_tensor("PA", [128, 1024], F32)),
          EE(nc.psum_tensor("PB", [128, 1024], F32))]
    PEn = EE(nc.psum_tensor("PEn", [D, D], F32))
    PM = EE(nc.psum_tensor("PM", [128, 512], F32))
    SL = EE(nc.semaphore("SL"))
    SGP = [EE(nc.semaphore("SGP0")), EE(nc.semaphore("SGP1")),
           EE(nc.semaphore("SGP2")), EE(nc.semaphore("SGP3"))]
    SYL = [EE(nc.semaphore("SYL0")), EE(nc.semaphore("SYL1"))]
    SYS = [EE(nc.semaphore("SYS0")), EE(nc.semaphore("SYS1"))]
    SWS = [EE(nc.semaphore("SWS0")), EE(nc.semaphore("SWS1"))]
    SUS = [EE(nc.semaphore("SUS0")), EE(nc.semaphore("SUS1"))]
    ST = EE(nc.semaphore("ST"))
    SS = EE(nc.semaphore("SS"))
    SV = EE(nc.semaphore("SV"))
    SP = EE(nc.semaphore("SP"))

    Q = {k: [] for k in "gystv"}
    C = {"l": 0, "g0": 0, "g1": 0, "g2": 0, "g3": 0,
         "y0": 0, "y1": 0, "z0": 0, "z1": 0,
         "w0": 0, "w1": 0, "u0": 0, "u1": 0,
         "s": 0, "t": 0, "v": 0, "p": 0}
    SEM = {"l": SL, "g0": SGP[0], "g1": SGP[1], "g2": SGP[2], "g3": SGP[3],
           "y0": SYL[0], "y1": SYL[1],
           "z0": SYS[0], "z1": SYS[1],
           "w0": SWS[0], "w1": SWS[1], "u0": SUS[0], "u1": SUS[1],
           "s": SS, "t": ST, "v": SV, "p": SP}

    def em(q, f):
        Q[q].append(f)

    def replay():
        with nc.Block() as block:
            @block.gpsimd
            def _(g):
                for f in Q['g']:
                    f(g)

            @block.sync
            def _(s_):
                for f in Q['y']:
                    f(s_)

            @block.tensor
            def _(t):
                for f in Q['t']:
                    f(t)

            @block.scalar
            def _(s):
                for f in Q['s']:
                    f(s)

            @block.vector
            def _(v):
                for f in Q['v']:
                    f(v)
        for k in Q:
            Q[k] = []

    def w(q, src, val):
        if val > 0:
            em(q, lambda e, s=SEM[src], v=val: e.wait_ge(s, v))

    def inc(q, n=1):
        C[q] += n
        return C[q]

    def fi(ins, fin, sem=None):
        if fin:
            ins.then_inc(sem if sem is not None else ST, 1)
        return ins

    QENG = {"l": "g", "g0": "g", "g1": "g", "g2": "g", "g3": "g",
            "y0": "y", "y1": "y",
            "z0": "y", "z1": "y",
            "w0": "s", "w1": "s", "u0": "s", "u1": "s",
            "s": "s", "t": "t", "v": "v", "p": "g"}

    def barrier():
        snap = dict(C)
        for q in "gystv":
            for o in C:
                if QENG[o] != q:
                    w(q, o, snap[o])

    def light_handoff(pe_gate):
        w('t', 's', C['s'])
        w('t', 'v', C['v'])
        w('s', 'w0', C['w0'])
        w('s', 'w1', C['w1'])
        w('v', 'w0', C['w0'])
        w('v', 'w1', C['w1'])
        w('g', 't', pe_gate)
        w('g', 'w0', C['w0'])
        w('g', 'w1', C['w1'])

    def aff_ident(t_):
        em('g', lambda g, t_=t_: g.memset(t_[:], 0.0).then_inc(SP, 1))
        v_ = inc('p', 1)
        w('g', 'p', v_)
        em('g', lambda g, t_=t_: g.affine_select(
            out=t_[:], in_=t_[:], compare_op=ALU.not_equal, fill=1.0, base=0,
            pattern=[[-1, 128]], channel_multiplier=1).then_inc(SP, 1))
        inc('p', 1)

    em('g', lambda g: g.memset(CST[:, 0:1], 1e-5))
    em('g', lambda g: g.memset(CST[:, 1:2], 1.0 / E))
    em('g', lambda g: g.memset(CST[:, 2:3], CTX_SCALE).then_inc(SP, 1))
    inc('p', 1)
    aff_ident(IDB)
    aff_ident(IDF)
    em('g', lambda g: g.dma_start(out=WKn[:], in_=wk).then_inc(SL, 16)); inc('l', 16)
    em('g', lambda g: g.dma_start(out=WVn[:], in_=wv).then_inc(SL, 16)); inc('l', 16)
    em('g', lambda g: g.dma_start(out=WQn[:], in_=wq).then_inc(SL, 16)); inc('l', 16)
    em('g', lambda g: g.dma_start(
        out=WOn[:].rearrange("p (a d) -> p a d", a=8),
        in_=wo.rearrange("(a p) d -> p a d", p=128)).then_inc(SL, 16)); inc('l', 16)
    for i, (dst, src) in enumerate([(SCL[0], s1v), (SCL[1], s2v), (SCL[2], sh1v),
                                    (SCL[3], sh2v), (SCL3, s3v)]):
        em('g', lambda g, d_=dst, s_=src: g.dma_start(out=d_[:], in_=s_).then_inc(SL, 16))
        inc('l', 16)
    g_loads = C['l']

    w('t', 'l', g_loads)
    w('t', 'p', C['p'])
    for wi, (wn, off) in enumerate([(WKn, 0), (WVn, 64), (WQn, 128)]):
        for c in range(EC):
            em('t', lambda t, wn=wn, c=c: fi(t.transpose(
                PT[0][0:128, c * 64:(c + 1) * 64], wn[0:D, c * 128:(c + 1) * 128],
                IDB[0:D, 0:D]), c == EC - 1))
        v_tp = inc('t')
        w('s', 't', v_tp)
        em('s', lambda s, off=off: s.copy(
            KVQT[:].rearrange("p (c t) -> p c t", c=EC)[:, :, off:off + 64],
            PT[0][:, 0:512].rearrange("p (c t) -> p c t", t=64)).then_inc(SS, 1))
        v_ev = inc('s')
        w('t', 's', v_ev)
    for c in range(EC):
        em('t', lambda t, c=c: fi(t.transpose(
            PT[1][0:D, c * 128:(c + 1) * 128], WOn[:, c * 64:(c + 1) * 64],
            IDB[:]), c == EC - 1))
    v_wot = inc('t')
    w('s', 't', v_wot)
    em('s', lambda s: s.copy(WOT[:], PT[1][0:D, 0:1024]).then_inc(SS, 1)); inc('s')
    for i in range(4):
        em('t', lambda t, i=i: t.transpose(
            PM[0:128, i * 32:(i + 1) * 32], SCL[i][0:32, :], IDF[0:32, 0:32]))
    em('t', lambda t: fi(t.transpose(PM[0:128, 4 * 32:4 * 32 + 8], SCL3[0:8, :],
                                     IDF[0:8, 0:8]), True))
    v_sct = inc('t')
    w('s', 't', v_sct)
    for i, dst in enumerate([SC1, SC2, SCH1, SCH2]):
        em('s', lambda s, d_=dst, i=i: s.copy(d_[:], PM[:, i * 32:(i + 1) * 32]))
    em('s', lambda s: s.copy(SC3[:], PM[:, 128:136]).then_inc(SS, 1)); inc('s')
    barrier()

    xtp = {}; xte = {}; kvq = {}; kve = {}; emm = {}
    xb_dma = {}
    KVQP = [PP[0][:, 0:192], PP[0][:, 512:704],
            PP[1][:, 0:192], PP[1][:, 512:704]]
    for t_i in range(NT + 4):
        if t_i < NT:
            r8 = t_i % 8
            rs = t_i % 4
            r2 = t_i % 2
            if t_i >= 8:
                w('g', 't', xtp[t_i - 8])
            if t_i >= 4:
                w('g', f'g{rs}', xb_dma[t_i - 4])
            em('g', lambda g, t_i=t_i, r8=r8, rs=rs: g.dma_start(
                out=XB[:, r8 * E:(r8 + 1) * E],
                in_=x[t_i * 128:(t_i + 1) * 128, :]).then_inc(SGP[rs], 16))
            xb_dma[t_i] = inc(f'g{rs}', 16)
            w('t', f'g{rs}', xb_dma[t_i])
            if t_i >= 2:
                w('t', 's', xte[t_i - 2])
            for c in range(EC):
                em('t', lambda t, r2=r2, r8=r8, c=c: fi(t.transpose(
                    PT[r2][:, c * 128:(c + 1) * 128],
                    XB[:, r8 * E + c * 128: r8 * E + (c + 1) * 128], IDB[:]),
                    c == EC - 1))
            xtp[t_i] = inc('t')
            w('s', 't', xtp[t_i])
            if t_i >= 8:
                w('s', 't', kvq[t_i - 8])
            em('s', lambda s, r2=r2, r8=r8: s.copy(
                XT[:, r8 * E:(r8 + 1) * E], PT[r2][:, 0:1024]).then_inc(SS, 1))
            xte[t_i] = inc('s')
        tj = t_i - 2
        if 0 <= tj < NT:
            rj = tj % 4
            rx = tj % 8
            W_ = 192 if tj < OT else 128
            w('t', 's', xte[tj])
            if tj >= 4:
                w('t', 'v', kve[tj - 4])
            for c in range(EC):
                em('t', lambda t, rj=rj, rx=rx, c=c, W_=W_: fi(t.matmul(
                    KVQP[rj][:, 0:W_],
                    XT[:, rx * E + c * 128: rx * E + (c + 1) * 128],
                    KVQT[:, c * 192: c * 192 + W_],
                    start=(c == 0), stop=(c == EC - 1)), c == EC - 1))
            kvq[tj] = inc('t')
            w('v', 't', kvq[tj])
            if tj >= 4:
                w('v', 't', emm[tj - 4])
            own_ = tj < OT
            em('v', lambda v, rj=rj, W_=W_, own_=own_: fi(v.tensor_copy(
                KVB[:, rj * 192: rj * 192 + W_], KVQP[rj][:, 0:W_]),
                not own_, SV))
            if own_:
                em('v', lambda v, tj=tj, rj=rj: fi(v.tensor_copy(
                    QN[:, tj * D:(tj + 1) * D], KVQP[rj][:, 128:192]), True, SV))
            kve[tj] = inc('v')
        tk = t_i - 4
        if 0 <= tk < NT:
            rk = tk % 4
            w('t', 'v', kve[tk])
            em('t', lambda t, rk=rk, tk=tk: t.matmul(
                PEn[:], KVB[:, rk * 192: rk * 192 + 64],
                KVB[:, rk * 192 + 64: rk * 192 + 128],
                start=(tk == 0), stop=(tk == NT - 1)).then_inc(ST, 1))
            emm[tk] = inc('t')
    barrier()

    em('s', lambda s: s.activation(ENF[:], PEn[:], AF.Copy, bias=0.0,
                                   scale=CST[0:D, 2:3]).then_inc(SS, 1))
    v_enf = inc('s')
    w('v', 's', v_enf)
    em('v', lambda v: v.tensor_copy(ENB[:], ENF[:]).then_inc(SV, 1)); v_enb = inc('v')
    for b in range(2):
        for k in range(8):
            ki = b * 8 + k
            em('t', lambda t, b=b, k=k, ki=ki: fi(t.transpose(
                PT[b][0:D, k * 128:(k + 1) * 128], QN[:, ki * D:(ki + 1) * D],
                IDB[:]), k == 7))
        v_qtp = inc('t')
        w('s', 't', v_qtp)
        em('s', lambda s, b=b: s.copy(
            QT[:, b * 1024:(b + 1) * 1024], PT[b][0:D, 0:1024]).then_inc(SS, 1))
        inc('s')
    v_qte = C['s']
    w('t', 'v', v_enb)
    w('t', 's', v_qte)
    for g_i in range(4):
        em('t', lambda t, g_i=g_i: t.matmul(
            PM[0:D, 0:512], ENB[:], QT[:, g_i * 512:(g_i + 1) * 512],
            start=True, stop=True).then_inc(ST, 1))
        v_ctm = inc('t')
        w('s', 't', v_ctm)
        em('s', lambda s, g_i=g_i: s.copy(
            CT[:, g_i * 512:(g_i + 1) * 512], PM[0:D, 0:512]).then_inc(SS, 1))
        v_cte = inc('s')
        w('t', 's', v_cte)
    barrier()

    def emit_ln(r, slot, dump=None):
        dump = TS1 if dump is None else dump
        w('s', 'v', C['v'])
        em('s', lambda s, slot=slot: s.activation(
            NM2[:, slot:slot + 1], NM[:, slot:slot + 1], AF.Copy, bias=0.0,
            scale=CST[:, 1:2]).then_inc(SS, 1))
        inc('s')
        em('s', lambda s, r=r, slot=slot, dump=dump: s.activation(
            dump[:, 0:1024], X1F[:, r * E:(r + 1) * E], AF.Square,
            bias=0.0, scale=1.0, accum_out=SSQ[:, slot:slot + 1]).then_inc(SS, 1))
        sq_ = inc('s')
        w('v', 's', sq_)
        em('v', lambda v, slot=slot: v.tensor_mul(
            MM[:, slot:slot + 1], NM2[:, slot:slot + 1],
            NM2[:, slot:slot + 1]).then_inc(SV, 1))
        vmm = inc('v')
        w('v', 'v', vmm)
        em('v', lambda v, slot=slot: v.scalar_tensor_tensor(
            VAR[:, slot:slot + 1], SSQ[:, slot:slot + 1], CST[:, 1:2],
            MM[:, slot:slot + 1], ALU.mult, ALU.subtract).then_inc(SV, 1))
        vv = inc('v')
        w('s', 'v', vv)
        em('s', lambda s, slot=slot: s.activation(
            SD[:, slot:slot + 1], VAR[:, slot:slot + 1], AF.Sqrt,
            bias=CST[:, 0:1], scale=1.0).then_inc(SS, 1))
        s2_ = inc('s')
        w('v', 's', s2_)
        em('v', lambda v, slot=slot: v.reciprocal(
            RS[:, slot:slot + 1], SD[:, slot:slot + 1]).then_inc(SV, 1))
        vr_ = inc('v')
        w('v', 'v', vr_)
        em('v', lambda v, slot=slot: v.tensor_scalar(
            NMR[:, slot:slot + 1], RS[:, slot:slot + 1], NM2[:, slot:slot + 1],
            -1.0, ALU.mult, ALU.mult).then_inc(SV, 1))
        vn_ = inc('v')
        w('v', 'v', vn_)
        em('v', lambda v, r=r, slot=slot: v.tensor_scalar(
            X1F[:, r * E:(r + 1) * E], X1F[:, r * E:(r + 1) * E],
            RS[:, slot:slot + 1], NMR[:, slot:slot + 1],
            ALU.mult, ALU.add).then_inc(SV, 1))
        return inc('v')

    def emit_b3(h):
        attn = {}; addv = {}; x1tp = {}; x1te = {}; xdma = {}; x1st = {}; lnvs = {}
        ysnap = {0: C['y0'], 1: C['y1']}
        zsnap = {0: C['z0'], 1: C['z1']}
        vsnap = C['v']
        for ti in range(OT // 2 + 1):
            tau = ti
            th = 8 * h + tau
            r = tau % 2
            if tau < 8:
                if tau >= 2:
                    w('y', 'v', addv[tau - 2])
                else:
                    w('y', f'y{r}', ysnap[r])
                    w('y', 'v', vsnap)
                em('y', lambda s_, th=th, r=r: s_.dma_start(
                    out=X1R[:, r * E:(r + 1) * E],
                    in_=x[th * 128:(th + 1) * 128, :]).then_inc(SYL[r], 16))
                xdma[tau] = inc(f'y{r}', 16)
                if tau >= 2:
                    w('t', 'v', addv[tau - 2])
                for g_i in range(2):
                    em('t', lambda t, th=th, r=r, g_i=g_i: fi(t.matmul(
                        PP[r][:, g_i * 512:(g_i + 1) * 512],
                        CT[:, th * 128:(th + 1) * 128],
                        WOT[:, g_i * 512:(g_i + 1) * 512],
                        start=True, stop=True), g_i == 1))
                attn[tau] = inc('t')
                w('v', 't', attn[tau])
                w('v', f'y{r}', xdma[tau])
                if tau >= 2:
                    w('v', f'z{r}', x1st[tau - 2])
                else:
                    w('v', f'z{r}', zsnap[r])
                em('v', lambda v, r=r: v.scalar_tensor_tensor(
                    X1F[:, r * E:(r + 1) * E], PP[r][:, 0:1024], 1.0,
                    X1R[:, r * E:(r + 1) * E], ALU.mult, ALU.add,
                    accum_out=NM[:, r:r + 1]).then_inc(SV, 1))
                inc('v')
                lnv = emit_ln(r, r)
                lnvs[tau] = lnv
                if tau >= 2:
                    w('v', 't', x1tp[tau - 2])
                w('v', 'v', lnv)
                em('v', lambda v, r=r: v.tensor_copy(
                    X1B[:, r * E:(r + 1) * E],
                    X1F[:, r * E:(r + 1) * E]).then_inc(SV, 1))
                addv[tau] = inc('v')
            tp = ti - 1
            if 0 <= tp < 8:
                rp = tp % 2
                w('t', 'v', addv[tp])
                if tp >= 2:
                    w('t', 's', x1te[tp - 2])
                for c in range(EC):
                    em('t', lambda t, rp=rp, c=c: fi(t.transpose(
                        PT[rp][:, c * 128:(c + 1) * 128],
                        X1B[:, rp * E + c * 128: rp * E + (c + 1) * 128],
                        IDB[:]), c == EC - 1))
                x1tp[tp] = inc('t')
                w('s', 't', x1tp[tp])
                em('s', lambda s, rp=rp, tp=tp: s.copy(
                    X1T[:].rearrange("p (c sl tk) -> p c sl tk", c=EC, sl=8)[:, :, tp, :],
                    PT[rp][:, 0:1024].rearrange("p (c tk) -> p c tk", c=EC)
                ).then_inc(SS, 1))
                x1te[tp] = inc('s')
                w('s', 'v', lnvs[tp])
                if tp >= 2:
                    w('s', f'z{rp}', x1st[tp - 2])
                else:
                    w('s', f'z{rp}', zsnap[rp])
                em('s', lambda s, tp=tp, rp=rp, h=h: s.dma_start(
                    out=x1s[(8 * h + tp) * 128:(8 * h + tp + 1) * 128, :],
                    in_=X1F[:, rp * E:(rp + 1) * E]).then_inc(SYS[rp], 16))
                x1st[tp] = inc(f'z{rp}', 16)
        return C['s']

    def emit_stage1(h, hook=None):
        wdma = {}; wtp = {}; wte = {}; wte_lo = {}; mmv = {}; sil = {}; hv = {}; cw = {}
        for oc in range(HC):
            if hook is not None:
                hook(oc)
            ro = oc % 2
            if h == 0:
                if oc >= 2:
                    w('g', 't', wtp[oc - 2])
                em('g', lambda g, oc=oc, ro=ro: g.dma_start(
                    out=WSB[:, ro * 1024:(ro + 1) * 1024],
                    in_=w1[oc * 128:(oc + 1) * 128, :]).then_inc(SGP[ro], 16))
                inc(f'g{ro}', 16)
                em('g', lambda g, oc=oc, ro=ro: g.dma_start(
                    out=WSB[:, 2048 + ro * 1024: 2048 + (ro + 1) * 1024],
                    in_=w2[oc * 128:(oc + 1) * 128, :]).then_inc(SGP[ro], 16))
                wdma[oc] = inc(f'g{ro}', 16)
                w('t', f'g{ro}', wdma[oc])
                if oc >= 1:
                    w('t', 's', wte[oc - 1])
                for c in range(EC):
                    em('t', lambda t, ro=ro, c=c: t.transpose(
                        PT[0][:, c * 128:(c + 1) * 128],
                        WSB[:, ro * 1024 + c * 128: ro * 1024 + (c + 1) * 128], IDB[:]))
                for c in range(EC):
                    em('t', lambda t, ro=ro, c=c: fi(t.transpose(
                        PT[1][:, c * 128:(c + 1) * 128],
                        WSB[:, 2048 + ro * 1024 + c * 128: 2048 + ro * 1024 + (c + 1) * 128],
                        IDB[:]), c == EC - 1))
                wtp[oc] = inc('t')
                w('s', 't', wtp[oc])
                if oc >= 2:
                    w('s', 't', mmv[(oc - 2, 1)])
                    w('s', f'w{ro}', cw[oc - 2])
                em('s', lambda s, ro=ro: s.copy(
                    WTB[:, ro * 2048: ro * 2048 + 1024],
                    PT[0][:, 0:1024]).then_inc(SS, 1))
                wte1 = inc('s')
                em('s', lambda s, ro=ro: s.copy(
                    WTB[:, ro * 2048 + 1024: ro * 2048 + 2048],
                    PT[1][:, 0:1024]).then_inc(SS, 1))
                wte[oc] = inc('s')
                wte_lo[oc] = wte1
                w('s', 's', wte[oc])
                em('s', lambda s, oc=oc, ro=ro: s.dma_start(
                    out=wt12[oc],
                    in_=WTB[:, ro * 2048:(ro + 1) * 2048]).then_inc(SWS[ro], 16))
                cw[oc] = inc(f'w{ro}', 16)
            else:
                if oc >= 2:
                    w('g', 't', mmv[(oc - 2, 1)])
                em('g', lambda g, oc=oc, ro=ro: g.dma_start(
                    out=WTB[:, ro * 2048:(ro + 1) * 2048],
                    in_=wt12[oc]).then_inc(SGP[ro], 16))
                wdma[oc] = inc(f'g{ro}', 16)
            for g_i in range(2):
                s_ = oc * 2 + g_i
                rp = s_ % 2
                if g_i == 0:
                    if h == 0:
                        w('t', 's', wte_lo[oc])
                    else:
                        w('t', f'g{ro}', wdma[oc])
                if s_ >= 2:
                    w('t', 's', sil[s_ - 2])
                    w('t', 'v', hv[s_ - 2])
                for c in range(EC):
                    em('t', lambda t, ro=ro, rp=rp, c=c, g_i=g_i: t.matmul(
                        PP[rp][:, 0:512],
                        WTB[:, ro * 2048 + c * 128: ro * 2048 + (c + 1) * 128],
                        X1T[:, c * 1024 + g_i * 512: c * 1024 + g_i * 512 + 512],
                        start=(c == 0), stop=(c == EC - 1)))
                if g_i == 0 and h == 0:
                    w('t', 's', wte[oc])
                for c in range(EC):
                    em('t', lambda t, ro=ro, rp=rp, c=c, g_i=g_i: fi(t.matmul(
                        PP[rp][:, 512:1024],
                        WTB[:, ro * 2048 + 1024 + c * 128: ro * 2048 + 1024 + (c + 1) * 128],
                        X1T[:, c * 1024 + g_i * 512: c * 1024 + g_i * 512 + 512],
                        start=(c == 0), stop=(c == EC - 1)), c == EC - 1))
                mmv[(oc, g_i)] = inc('t')
                w('s', 't', mmv[(oc, g_i)])
                em('s', lambda s, rp=rp, g_i=g_i, oc=oc: s.activation(
                    TS1[:, g_i * 512:(g_i + 1) * 512], PP[rp][:, 0:512], AF.Silu,
                    bias=0.0, scale=SC1[:, oc:oc + 1]).then_inc(SS, 1))
                sil[s_] = inc('s')
                w('v', 's', sil[s_])
                em('v', lambda v, rp=rp, g_i=g_i, oc=oc: v.scalar_tensor_tensor(
                    HB[:, oc * 1024 + g_i * 512: oc * 1024 + g_i * 512 + 512],
                    PP[rp][:, 512:1024], SC2[:, oc:oc + 1],
                    TS1[:, g_i * 512:(g_i + 1) * 512],
                    ALU.mult, ALU.mult).then_inc(SV, 1))
                hv[s_] = inc('v')
        return wtp.get(HC - 1, C['t'])

    def emit_big_stage(wsrc, scl, src_buf, dst_buf, nout, func, cache, h):
        wdma = {}; wtp = {}; wte = {}; mmv = {}; gel = {}; cw = {}; rdv = {}
        for oc in range(nout):
            so = oc % 2
            if h == 0:
                for half in range(2):
                    s_slot = (oc * 2 + half) % 2
                    if oc >= 1:
                        w('g', 't', wtp[(oc - 1, 2 * half + 1)])
                    em('g', lambda g, oc=oc, half=half, s_slot=s_slot, wsrc=wsrc:
                       g.dma_start(
                        out=WSB[:, s_slot * 2048:(s_slot + 1) * 2048],
                        in_=wsrc[oc * 128:(oc + 1) * 128,
                                 half * 2048:(half + 1) * 2048]).then_inc(SGP[s_slot], 16))
                    wdma[(oc, half)] = inc(f'g{s_slot}', 16)
            else:
                if oc >= 2:
                    w('g', 't', mmv[(oc - 2, 3)])
                em('g', lambda g, oc=oc, so=so, cache=cache: g.dma_start(
                    out=WTB[:, so * H:(so + 1) * H],
                    in_=cache[oc]).then_inc(SGP[so], 16))
                rdv[oc] = inc(f'g{so}', 16)

            def emit_tp(b):
                half = b // 2
                s_slot = (oc * 2 + half) % 2
                w('t', f'g{s_slot}', wdma[(oc, half)])
                if b >= 2:
                    w('t', 'v', wte[(oc, b - 2)])
                elif oc >= 1:
                    w('t', 'v', wte[(oc - 1, b + 2)])
                for j in range(8):
                    em('t', lambda t, s_slot=s_slot, b=b, j=j: fi(t.transpose(
                        PT[b % 2][:, j * 128:(j + 1) * 128],
                        WSB[:, s_slot * 2048 + (b % 2) * 1024 + j * 128:
                            s_slot * 2048 + (b % 2) * 1024 + (j + 1) * 128],
                        IDB[:]), j == 7))
                wtp[(oc, b)] = inc('t')
                w('v', 't', wtp[(oc, b)])
                if oc >= 2:
                    w('v', 't', mmv[(oc - 2, b)])
                    if b == 0:
                        w('v', f'w{so}', cw[oc - 2])
                em('v', lambda v, so=so, b=b: v.tensor_copy(
                    WTB[:, so * H + b * 1024: so * H + (b + 1) * 1024],
                    PT[b % 2][:, 0:1024]).then_inc(SV, 1))
                wte[(oc, b)] = inc('v')
                if b == 3:
                    w('s', 'v', wte[(oc, 3)])
                    em('s', lambda s, oc=oc, so=so, cache=cache: s.dma_start(
                        out=cache[oc],
                        in_=WTB[:, so * H:(so + 1) * H]).then_inc(SWS[so], 16))
                    cw[oc] = inc(f'w{so}', 16)

            def emit_mm(b):
                rp = oc % 2
                if h == 0:
                    w('t', 'v', wte[(oc, b)])
                elif b == 0:
                    w('t', f'g{so}', rdv[oc])
                if b == 0 and oc >= 2:
                    w('t', 's', gel[oc - 2])
                for j in range(8):
                    ic = b * 8 + j
                    first = (b == 0 and j == 0)
                    last = (b == 3 and j == 7)
                    for g_i in range(2):
                        em('t', lambda t, rp=rp, so=so, b=b, j=j, ic=ic, g_i=g_i,
                           first=first, last=last, src_buf=src_buf: fi(t.matmul(
                            PP[rp][:, g_i * 512:(g_i + 1) * 512],
                            WTB[:, so * H + b * 1024 + j * 128:
                                so * H + b * 1024 + (j + 1) * 128],
                            src_buf[:, ic * 1024 + g_i * 512:
                                    ic * 1024 + g_i * 512 + 512],
                            start=first, stop=last), j == 7 and g_i == 1))
                mmv[(oc, b)] = inc('t')

            if h == 0:
                emit_tp(0)
                emit_tp(1)
                emit_tp(2)
                emit_mm(0)
                emit_tp(3)
                emit_mm(1)
                emit_mm(2)
                emit_mm(3)
            else:
                for b in range(4):
                    emit_mm(b)
            rp = oc % 2
            w('s', 't', mmv[(oc, 3)])
            em('s', lambda s, rp=rp, oc=oc, scl=scl, dst_buf=dst_buf, func=func:
               s.activation(dst_buf[:, oc * 1024:(oc + 1) * 1024],
                            PP[rp][:, 0:1024], func, bias=0.0,
                            scale=scl[:, oc:oc + 1]).then_inc(SS, 1))
            gel[oc] = inc('s')
        return wtp.get((nout - 1, 3), C['t'])

    def make_final_ln(h, two_buf=False):
        st = {"ftp": {}, "fte": {}, "addv": {}, "xdma": {}, "outst": {},
              "lnvs": {}, "pend": None}
        nb = 2 if two_buf else 1
        ysnap = {0: C['y0'], 1: C['y1']}
        zsnap = {0: C['z0'], 1: C['z1']}
        vsnap = C['v']

        def step(tau):
            ftp, fte, addv, xdma, outst = (st["ftp"], st["fte"], st["addv"],
                                           st["xdma"], st["outst"])
            pt = st["pend"]
            if pt is not None:
                pr = pt % 2
                w('s', 'v', st["lnvs"][pt])
                if pt >= 2:
                    w('s', f'z{pr}', outst[pt - 2])
                else:
                    w('s', f'z{pr}', zsnap[pr])
                em('s', lambda s, th2=8 * h + pt, pr=pr: s.dma_start(
                    out=y[th2 * 128:(th2 + 1) * 128, :],
                    in_=X1F[:, pr * E:(pr + 1) * E]).then_inc(SYS[pr], 16))
                outst[pt] = inc(f'z{pr}', 16)
                st["pend"] = None
            if tau >= 8:
                return
            th = 8 * h + tau
            r = tau % 2
            if tau >= 2:
                w('t', 's', fte[tau - 2])
            for c in range(EC):
                em('t', lambda t, r=r, c=c, tau=tau: fi(t.transpose(
                    PT[r][:, c * 128:(c + 1) * 128],
                    GB[:, c * 1024 + tau * 128: c * 1024 + (tau + 1) * 128],
                    IDB[:]), c == EC - 1))
            ftp[tau] = inc('t')
            buf = FT if (not two_buf or tau % 2 == 0) else TS1
            w('s', 't', ftp[tau])
            if tau >= nb:
                w('s', 'v', addv[tau - nb])
            em('s', lambda s, r=r, buf=buf: s.copy(
                buf[:, 0:1024], PT[r][:, 0:1024]).then_inc(SS, 1))
            fte[tau] = inc('s')
            if tau >= 2:
                w('y', 'v', addv[tau - 2])
            else:
                w('y', f'y{r}', ysnap[r])
                w('y', 'v', vsnap)
                w('y', f'z{r}', zsnap[r])
            em('y', lambda s_, th=th, r=r: s_.dma_start(
                out=X1R[:, r * E:(r + 1) * E],
                in_=x1s[th * 128:(th + 1) * 128, :]).then_inc(SYL[r], 16))
            xdma[tau] = inc(f'y{r}', 16)
            w('v', 's', fte[tau])
            w('v', f'y{r}', xdma[tau])
            if tau >= 2:
                w('v', f'z{r}', outst[tau - 2])
            else:
                w('v', f'z{r}', zsnap[r])
            em('v', lambda v, r=r, buf=buf: v.scalar_tensor_tensor(
                X1F[:, r * E:(r + 1) * E], X1R[:, r * E:(r + 1) * E], 1.0,
                buf[:, 0:1024], ALU.mult, ALU.add,
                accum_out=NM[:, r:r + 1]).then_inc(SV, 1))
            inc('v')
            lnv = emit_ln(r, r, dump=buf)
            addv[tau] = lnv
            st["lnvs"][tau] = lnv
            st["pend"] = tau

        return step

    replay()
    ctxA.close()

    X1T = EE(nc.sbuf_tensor("X1T", [128, EC * 1024], BF16))
    HB = EE(nc.sbuf_tensor("HB", [128, HC * 1024], BF16))
    GB = EE(nc.sbuf_tensor("GB", [128, HC * 1024], BF16))
    WSB = EE(nc.sbuf_tensor("WSB", [128, H], BF16))
    WTB = EE(nc.sbuf_tensor("WTB", [128, 2 * H], BF16))
    TS1 = EE(nc.sbuf_tensor("TS1", [128, 1024], F32))
    FT = EE(nc.sbuf_tensor("FT", [128, 1024], F32))
    X1R = EE(nc.sbuf_tensor("X1R", [128, 2 * E], F32))
    X1F = EE(nc.sbuf_tensor("X1F", [128, 2 * E], F32))
    X1B = EE(nc.sbuf_tensor("X1B", [128, 2 * E], BF16))
    NM = EE(nc.sbuf_tensor("NM", [128, 2], F32))
    MM = EE(nc.sbuf_tensor("MM", [128, 2], F32))
    VAR = EE(nc.sbuf_tensor("VAR", [128, 2], F32))
    NMR = EE(nc.sbuf_tensor("NMR", [128, 2], F32))
    NM2 = EE(nc.sbuf_tensor("NM2", [128, 2], F32))
    SSQ = EE(nc.sbuf_tensor("SSQ", [128, 2], F32))
    SD = EE(nc.sbuf_tensor("SD", [128, 2], F32))
    RS = EE(nc.sbuf_tensor("RS", [128, 2], F32))
    for h in range(2):
        pre_b3_gate = C['t']
        emit_b3(h)
        light_handoff(pre_b3_gate)
        if h == 0:
            g1_ = emit_stage1(0)
        else:
            fstep0 = make_final_ln(0)
            g1_ = emit_stage1(1, hook=lambda oc: fstep0(oc // 4) if oc % 4 == 0 else None)
            fstep0(8)
        light_handoff(g1_)
        g2_ = emit_big_stage(h1, SCH1, HB, GB, HC, AF.Gelu, h1t, h)
        light_handoff(g2_)
        g3_ = emit_big_stage(h2, SCH2, GB, HB, HC, AF.Gelu, h2t, h)
        light_handoff(g3_)
        g4_ = emit_big_stage(w3, SC3, HB, GB, EC, AF.Copy, w3t, h)
        if h == 0:
            light_handoff(g4_)
        else:
            barrier()
    fstep1 = make_final_ln(1, two_buf=True)
    for tau in range(9):
        fstep1(tau)
    barrier()

    em('y', lambda s_: s_.wait_ge(SYS[0], C['z0']))
    em('y', lambda s_: s_.wait_ge(SYS[1], C['z1']))
    replay()

    _CACHE['ctx'] = ctx
    return nc


def kernel(**inputs):
    def arr(k):
        return np.ascontiguousarray(np.asarray(inputs[k], dtype=np.float32))

    x = arr("L0_attn_x").reshape(NTOK, E)
    base = {
        "wq": arr("L0_attn_wq"), "wk": arr("L0_attn_wk"),
        "wv": arr("L0_attn_wv"), "wo": arr("L0_attn_wo"),
        "w1": arr("L0_ffn_w1_w"), "w2": arr("L0_ffn_w2_w"),
        "h1": arr("L0_ffn_h1_w"), "h2": arr("L0_ffn_h2_w"),
        "w3": arr("L0_ffn_w3_w"),
        "s1v": arr("L0_ffn_w1_s").reshape(32, 128),
        "s2v": arr("L0_ffn_w2_s").reshape(32, 128),
        "sh1v": arr("L0_ffn_h1_s").reshape(32, 128),
        "sh2v": arr("L0_ffn_h2_s").reshape(32, 128),
        "s3v": arr("L0_ffn_w3_s").reshape(8, 128),
    }
    if 'nc' not in _CACHE:
        _CACHE['nc'] = build()
    nc = _CACHE['nc']
    in_maps = []
    for c in range(8):
        m = dict(base)
        m["x"] = np.roll(x, -c * OWN, axis=0) if c else x
        in_maps.append(m)
    res = run_bass_kernel_spmd(nc, in_maps, core_ids=list(range(8)))
    out = np.concatenate([np.asarray(res.results[c]["y"]) for c in range(8)], axis=0)
    return out.reshape(1, NTOK, E).astype(np.float32)
